# version 02 (final)
# baseline (speedup 1.0000x reference)
import numpy as np
import ml_dtypes
from contextlib import ExitStack

import concourse.bass as bass
import concourse.mybir as mybir
from concourse.bass_utils import run_bass_kernel_spmd

F32 = mybir.dt.float32
BF16 = mybir.dt.bfloat16
I32 = mybir.dt.int32
AF = mybir.ActivationFunctionType
ALU = mybir.AluOpType
AX = mybir.AxisListType

T = 2048
D = 1024
DT = D // 128
NTB = T // 512
EPS = 1e-5

BF = ml_dtypes.bfloat16


def bank(ps_all, b, w=512):
    return ps_all[:, b * 512: b * 512 + w]


DETECT_RACES = True


class Prog:

    def __init__(self):
        self.nc = bass.Bass("TRN2", target_bir_lowering=False,
                            detect_race_conditions=DETECT_RACES)
        self.es = ExitStack()
        self.sem_vals = {}

    def sb(self, name, shape, dt=F32):
        return self.es.enter_context(self.nc.sbuf_tensor(name, shape, dt))

    def psum(self, name, shape, dt=F32):
        return self.es.enter_context(self.nc.psum_tensor(name, shape, dt))

    def sem(self, name):
        h = self.es.enter_context(self.nc.semaphore(name))
        self.sem_vals[h] = 0
        return h

    def inc(self, inst, sem, n=1):
        inst.then_inc(sem, n)
        self.sem_vals[sem] += n
        return inst

    def bump(self, sem, n):
        self.sem_vals[sem] += n

    def val(self, sem):
        return self.sem_vals[sem]

    def wait(self, eng, sem, val=None):
        eng.wait_ge(sem, self.sem_vals[sem] if val is None else val)



class LnBufs:

    def __init__(self, p, ps_all, tbw=512, tb=None, sq=None, rows=None, scr=None):
        self.tbw = tbw
        self.tb = tb if tb is not None else p.sb("ln_tb", [128, DT, tbw], BF16)
        self.sq = sq if sq is not None else p.sb("ln_sq", [128, DT, tbw], BF16)
        self.scr = scr if scr is not None else p.sb("ln_scr", [128, 2, tbw], F32)
        self.rows = rows if rows is not None else p.sb("ln_rows", [1, 4 * tbw], F32)
        self.ones_col_bf = p.sb("ones_col_bf", [128, 1], BF16)
        self.ones_row_f32 = p.sb("ones_row_f32", [1, 128], F32)
        self.cnm05 = p.sb("cnm05", [128, 1], F32)
        self.ps_all = ps_all

    def rowm(self):
        return self.rows[0:1, 0:self.tbw]

    def rowv(self):
        return self.rows[0:1, self.tbw:2 * self.tbw]

    def rowr(self):
        return self.rows[0:1, 2 * self.tbw:3 * self.tbw]

    def rowb(self):
        return self.rows[0:1, 3 * self.tbw:4 * self.tbw]


def emit_consts(p, block, lb, ve):
    @block.vector
    def _(vector):
        vector.memset(lb.ones_col_bf[:], 1.0)
        vector.memset(lb.ones_row_f32[:], 1.0)
        p.inc(vector.memset(lb.cnm05[:], -0.5), ve)


def emit_ln_chunk(p, block, lb, tin, tout, gamma_row=None, beta_col=None,
                  sems=None, stat_banks=(0, 1), bc_banks=(2, 3), W=512):
    pe, se, ve = sems["pe"], sems["se"], sems["ve"]
    PS = lb.ps_all
    s1 = bank(PS, stat_banks[0], W)[0:1, :]
    s2 = bank(PS, stat_banks[1], W)[0:1, :]
    A_bc = bank(PS, bc_banks[0], W)
    B_bc = bank(PS, bc_banks[1], W)

    @block.scalar
    def _(scalar):
        p.wait(scalar, ve)
        p.wait(scalar, pe)
        scalar.activation(lb.tb[:, :, 0:W], tin, AF.Copy)
        p.inc(scalar.activation(lb.sq[:, :, 0:W], tin, AF.Square), se)

    @block.tensor
    def _(tensor):
        p.wait(tensor, se)
        p.wait(tensor, ve)
        for dt_ in range(DT):
            tensor.matmul(s1, lb.ones_col_bf[:, 0:1], lb.tb[:, dt_, 0:W],
                          start=(dt_ == 0), stop=(dt_ == DT - 1))
        for dt_ in range(DT):
            i = tensor.matmul(s2, lb.ones_col_bf[:, 0:1], lb.sq[:, dt_, 0:W],
                              start=(dt_ == 0), stop=(dt_ == DT - 1))
        p.inc(i, pe)

    @block.vector
    def _(vector):
        p.wait(vector, pe)
        p.inc(vector.tensor_single_scalar(lb.rowm(), s1, 1.0 / D, ALU.mult), ve)
        vector.tensor_single_scalar(lb.rowv(), s2, 1.0 / D, ALU.mult)
        p.wait(vector, ve)
        p.inc(vector.scalar_tensor_tensor(lb.scr[0:1, 0, 0:W], lb.rowm(), -1.0,
                                          lb.rowm(), ALU.mult, ALU.mult), ve)
        p.wait(vector, ve)
        p.inc(vector.scalar_tensor_tensor(lb.rowv(), lb.scr[0:1, 0, 0:W], EPS,
                                          lb.rowv(), ALU.add, ALU.add), ve)

    @block.scalar
    def _(scalar):
        p.wait(scalar, ve)
        p.inc(scalar.activation(lb.rowr(), lb.rowv(), AF.Ln), se)
        p.wait(scalar, se)
        p.inc(scalar.activation(lb.rowr(), lb.rowr(), AF.Exp,
                                scale=lb.cnm05[0:1, 0:1]), se)

    @block.vector
    def _(vector):
        p.wait(vector, se)
        p.inc(vector.scalar_tensor_tensor(lb.rowb(), lb.rowm(), -1.0,
                                          lb.rowr(), ALU.mult, ALU.mult), ve)

    if gamma_row is None:
        @block.tensor
        def _(tensor):
            p.wait(tensor, ve)
            tensor.matmul(A_bc, lb.ones_row_f32[0:1, :], lb.rowr(),
                          start=True, stop=True)
            p.inc(tensor.matmul(B_bc, lb.ones_row_f32[0:1, :], lb.rowb(),
                                start=True, stop=True), pe)

        @block.vector
        def _(vector):
            p.wait(vector, pe)
            ve0 = p.val(ve)
            marks = {}
            for dt_ in range(DT):
                marks[dt_] = p.val(ve) + 1
                p.inc(vector.tensor_tensor(lb.scr[:, dt_ % 2, 0:W],
                                           tin[:, dt_, :], A_bc, ALU.mult), ve)
                if dt_ >= 1:
                    d0 = dt_ - 1
                    vector.wait_ge(ve, marks[d0])
                    p.inc(vector.tensor_tensor(tout[:, d0, :],
                                               lb.scr[:, d0 % 2, 0:W], B_bc,
                                               ALU.add), ve)
            d0 = DT - 1
            vector.wait_ge(ve, marks[d0])
            p.inc(vector.tensor_tensor(tout[:, d0, :], lb.scr[:, d0 % 2, 0:W],
                                       B_bc, ALU.add), ve)
    else:
        bcp = [bc_banks, (stat_banks[0], stat_banks[1])]
        marks = {}
        for dt_ in range(DT):
            Ab = bank(PS, bcp[dt_ % 2][0], W)
            Bb = bank(PS, bcp[dt_ % 2][1], W)

            @block.tensor
            def _(tensor, dt_=dt_, Ab=Ab, Bb=Bb):
                if dt_ < 2:
                    p.wait(tensor, ve)
                else:
                    tensor.wait_ge(sems["ve"], marks[dt_ - 2] + 1)
                tensor.matmul(Ab, gamma_row[0:1, dt_ * 128:(dt_ + 1) * 128],
                              lb.rowr(), start=True, stop=True)
                p.inc(tensor.matmul(Bb, gamma_row[0:1, dt_ * 128:(dt_ + 1) * 128],
                                    lb.rowb(), start=True, stop=True), pe)

            @block.vector
            def _(vector, dt_=dt_, Ab=Ab, Bb=Bb):
                p.wait(vector, pe)
                marks[dt_] = p.val(ve) + 1
                p.inc(vector.tensor_tensor(lb.scr[:, dt_ % 2, 0:W],
                                           tin[:, dt_, :], Ab, ALU.mult), ve)
                vector.wait_ge(ve, marks[dt_])
                p.inc(vector.scalar_tensor_tensor(
                    tout[:, dt_, :], lb.scr[:, dt_ % 2, 0:W],
                    beta_col[:, dt_:dt_ + 1], Bb, ALU.add, ALU.add), ve)



def build_l1_ffn():
    p = Prog()
    nc = p.nc
    x_in = nc.dram_tensor("x_in", [D, T], F32, kind="ExternalInput").ap()
    w1_h = nc.dram_tensor("w1", [D, 4096], F32, kind="ExternalInput").ap()
    b1c_h = nc.dram_tensor("b1c", [128, 32], F32, kind="ExternalInput").ap()
    w2_h = nc.dram_tensor("w2", [4096, D], F32, kind="ExternalInput").ap()
    b2c_h = nc.dram_tensor("b2c", [128, 8], F32, kind="ExternalInput").ap()
    gam_h = nc.dram_tensor("gam", [1, D], F32, kind="ExternalInput").ap()
    bet_h = nc.dram_tensor("betc", [128, 8], F32, kind="ExternalInput").ap()
    x_out = nc.dram_tensor("x_out", [D, T], F32, kind="ExternalOutput").ap()

    NW2 = 6
    xv = x_in.rearrange("(dt p) t -> p dt t", p=128)
    xo = x_out.rearrange("(dt p) t -> p dt t", p=128)

    with p.es:
        w1b = p.sb("w1b", [128, DT, 4096], BF16)
        w2r = p.sb("w2r", [128, NW2, D], BF16)
        h = p.sb("h", [128, 32, 512], BF16)
        xc = p.sb("xc", [128, 2, DT, 512], F32)
        xbc = p.sb("xbc", [128, DT, 512], BF16)
        t1 = p.sb("t1", [128, DT, 512], F32)
        b1c = p.sb("b1c_s", [128, 32], F32)
        b2c = p.sb("b2c_s", [128, 8], F32)
        gam = p.sb("gam_s", [1, D], F32)
        betc = p.sb("betc_s", [128, 8], F32)
        ps_all = p.psum("ps_all", [128, 4096], F32)
        lb = LnBufs(p, ps_all)

        ld = p.sem("ld")
        lg = p.sem("lg")
        lx = p.sem("lx")
        lw2 = p.sem("lw2")
        pe = p.sem("pe")
        se = p.sem("se")
        ve = p.sem("ve")
        so = p.sem("so")
        sems = {"pe": pe, "se": se, "ve": ve}

        block = p.es.enter_context(nc.Block())

        @block.sync
        def _(sync):
            p.inc(sync.dma_start(out=b1c[:], in_=b1c_h[:, :]), ld, 16)
            p.inc(sync.dma_start(out=b2c[:], in_=b2c_h[:, :]), ld, 16)
            p.inc(sync.dma_start(out=gam[:], in_=gam_h[:, :]), ld, 16)
            p.inc(sync.dma_start(out=betc[:], in_=bet_h[:, :]), ld, 16)
            for tb in range(min(2, NTB)):
                p.inc(sync.dma_start(out=xc[:, tb],
                                     in_=xv[:, :, tb * 512:(tb + 1) * 512]), lx, 16)

        @block.gpsimd
        def _(g):
            for dt_ in range(DT):
                p.inc(g.dma_start(out=w1b[:, dt_, :],
                                  in_=w1_h[dt_ * 128:(dt_ + 1) * 128, :]), lg, 16)

        ld_all = p.val(ld)
        lg_all = p.val(lg)

        emit_consts(p, block, lb, ve)

        p2_pe_base = {}

        for tb in range(NTB):
            cur = xc[:, tb % 2]

            @block.scalar
            def _(scalar, tb=tb):
                p.wait(scalar, lx, 16 * (tb + 1))
                p.wait(scalar, pe)
                p.inc(scalar.activation(xbc[:], cur, AF.Copy), se)

            pe0 = p.val(pe)
            secast = p.val(se)

            @block.tensor
            def _(tensor, pe0=pe0, secast=secast):
                p.wait(tensor, ld, ld_all)
                p.wait(tensor, lg, lg_all)
                tensor.wait_ge(se, secast)
                for ft in range(32):
                    if ft >= 2:
                        tensor.wait_ge(se, secast + (ft - 1))
                    bk = bank(ps_all, ft % 2)
                    for dt_ in range(DT):
                        i = tensor.matmul(bk, w1b[:, dt_, ft * 128:(ft + 1) * 128],
                                          xbc[:, dt_, :],
                                          start=(dt_ == 0), stop=(dt_ == DT - 1))
                    i.then_inc(pe, 1)
            p.bump(pe, 32)

            @block.scalar
            def _(scalar, pe0=pe0):
                for ft in range(32):
                    scalar.wait_ge(pe, pe0 + ft + 1)
                    scalar.activation(h[:, ft, :], bank(ps_all, ft % 2), AF.Relu,
                                      bias=b1c[:, ft:ft + 1]).then_inc(se, 1)
            p.bump(se, 32)

            lw2_0 = p.val(lw2)
            p2_pe_base[tb] = p.val(pe)

            @block.gpsimd
            def _(g, tb=tb):
                for ft in range(32):
                    gi = tb * 32 + ft
                    if gi >= NW2:
                        done_tb, done_ft = divmod(gi - NW2, 32)
                        g.wait_ge(pe, p2_pe_base[done_tb] + done_ft + 1)
                    g.dma_start(out=w2r[:, gi % NW2, :],
                                in_=w2_h[ft * 128:(ft + 1) * 128, :]).then_inc(lw2, 16)
            p.bump(lw2, 32 * 16)

            se1 = p.val(se)

            @block.tensor
            def _(tensor, lw2_0=lw2_0, se1=se1):
                tensor.wait_ge(se, se1)
                p.wait(tensor, ve)
                for ft in range(32):
                    tensor.wait_ge(lw2, lw2_0 + 16 * (ft + 1))
                    for mt in range(DT):
                        i = tensor.matmul(bank(ps_all, mt),
                                          w2r[:, (tb * 32 + ft) % NW2, mt * 128:(mt + 1) * 128],
                                          h[:, ft, :],
                                          start=(ft == 0), stop=(ft == 31))
                    i.then_inc(pe, 1)
            p.bump(pe, 32)

            @block.vector
            def _(vector):
                p.wait(vector, pe)
                for mt in range(DT):
                    i = vector.scalar_tensor_tensor(
                        t1[:, mt, :], bank(ps_all, mt), b2c[:, mt:mt + 1],
                        cur[:, mt, :], ALU.add, ALU.add)
                p.inc(i, ve)

            emit_ln_chunk(p, block, lb, t1[:], t1[:], gamma_row=gam[0:1, :],
                          beta_col=betc, sems=sems)

            @block.vector
            def _(vector):
                p.wait(vector, ve)
                for mt in range(DT):
                    i = vector.tensor_tensor(cur[:, mt, :], cur[:, mt, :],
                                             t1[:, mt, :], ALU.add)
                p.inc(i, ve)

            emit_ln_chunk(p, block, lb, cur[:], cur[:], sems=sems)

            @block.sync
            def _(sync, tb=tb):
                p.wait(sync, ve)
                p.inc(sync.dma_start(out=xo[:, :, tb * 512:(tb + 1) * 512],
                                     in_=cur[:]), so, 16)
                if tb + 2 < NTB:
                    p.wait(sync, so)
                    p.inc(sync.dma_start(
                        out=xc[:, tb % 2],
                        in_=xv[:, :, (tb + 2) * 512:(tb + 3) * 512]), lx, 16)

        @block.sync
        def _(sync):
            p.wait(sync, so)
    return p



def build_l2_ffn(debug=0):
    p = Prog()
    nc = p.nc
    x_in = nc.dram_tensor("x_in", [D, T], F32, kind="ExternalInput").ap()
    w1_h = nc.dram_tensor("w1", [4096, D], F32, kind="ExternalInput").ap()
    w2t_h = nc.dram_tensor("w2t", [4096, D], F32, kind="ExternalInput").ap()
    b1c_h = nc.dram_tensor("b1c", [128, 32], F32, kind="ExternalInput").ap()
    b2c_h = nc.dram_tensor("b2c", [128, 8], F32, kind="ExternalInput").ap()
    x_out = nc.dram_tensor("x_out", [D, T], F32, kind="ExternalOutput").ap()
    if debug:
        dbg_wct = nc.dram_tensor("dbg_wct", [D, D], BF16, kind="ExternalOutput").ap()
        dbg_bcc = nc.dram_tensor("dbg_bcc", [128, 8], F32, kind="ExternalOutput").ap()

    NR = 8
    xv = x_in.rearrange("(dt p) t -> p dt t", p=128)
    xo = x_out.rearrange("(dt p) t -> p dt t", p=128)

    with p.es:
        w2tb = p.sb("w2tb", [128, 32, D], BF16)
        w1r = p.sb("w1r", [128, NR, D], BF16)
        wct = p.sb("wct", [128, DT, D], BF16)
        b1cb = p.sb("b1cb_s", [128, 32], BF16)
        b2cs = p.sb("b2cs_s", [128, 8], F32)
        bcc = p.sb("bcc_s", [128, 8], F32)
        xc = p.sb("xc", [128, 2, DT, 512], F32)
        xbc = p.sb("xbc", [128, DT, 512], BF16)
        ps_all = p.psum("ps_all", [128, 4096], F32)
        lb = LnBufs(p, ps_all)

        ld = p.sem("ld")
        lg = p.sem("lg")
        lw1 = p.sem("lw1")
        lx = p.sem("lx")
        pe = p.sem("pe")
        se = p.sem("se")
        ve = p.sem("ve")
        so = p.sem("so")
        sems = {"pe": pe, "se": se, "ve": ve}

        block = p.es.enter_context(nc.Block())

        @block.sync
        def _(sync):
            for tb in range(2):
                p.inc(sync.dma_start(out=xc[:, tb],
                                     in_=xv[:, :, tb * 512:(tb + 1) * 512]), lx, 16)
            p.inc(sync.dma_start(out=b2cs[:], in_=b2c_h[:, :]), ld, 16)

        @block.gpsimd
        def _(g):
            for ft in range(32):
                p.inc(g.dma_start(out=w2tb[:, ft, :],
                                  in_=w2t_h[ft * 128:(ft + 1) * 128, :]), lg, 16)
            p.inc(g.dma_start(out=b1cb[:], in_=b1c_h[:, :]), lg, 16)

        ld_all = p.val(ld)
        lg_all = p.val(lg)
        emit_consts(p, block, lb, ve)

        pe_w1_base = p.val(pe)

        if debug != 3:
            @block.gpsimd
            def _(g):
                for gi in range(64):
                    f = gi % 32
                    if gi >= NR:
                        g.wait_ge(pe, pe_w1_base + (gi - NR) + 1)
                    g.dma_start(out=w1r[:, gi % NR, :],
                                in_=w1_h[f * 128:(f + 1) * 128, :]).then_inc(lw1, 16)
            p.bump(lw1, 64 * 16)

        for q in ([] if debug == 3 else range(2)):
            se_evac = p.val(se)

            @block.tensor
            def _(tensor, q=q):
                p.wait(tensor, ld, ld_all)
                p.wait(tensor, lg, lg_all)
                if q == 1:
                    p.wait(tensor, se)
                for f in range(32):
                    gi = q * 32 + f
                    tensor.wait_ge(lw1, 16 * (gi + 1))
                    for j in range(DT):
                        dloc, mh = divmod(j, 2)
                        i = tensor.matmul(
                            bank(ps_all, j),
                            w1r[:, gi % NR, (q * 4 + dloc) * 128:(q * 4 + dloc + 1) * 128],
                            w2tb[:, f, mh * 512:(mh + 1) * 512],
                            start=(f == 0), stop=(f == 31))
                    i.then_inc(pe, 1)
            p.bump(pe, 32)

            @block.scalar
            def _(scalar, q=q):
                p.wait(scalar, pe)
                for j in range(DT):
                    dloc, mh = divmod(j, 2)
                    i = scalar.activation(
                        wct[:, q * 4 + dloc, mh * 512:(mh + 1) * 512],
                        bank(ps_all, j), AF.Copy)
                p.inc(i, se)

        @block.tensor
        def _(tensor):
            p.wait(tensor, se)
            p.wait(tensor, ld, ld_all)
            for mt in range(DT):
                for f in range(32):
                    i = tensor.matmul(ps_all[:, mt:mt + 1],
                                      w2tb[:, f, mt * 128:(mt + 1) * 128],
                                      b1cb[:, f:f + 1],
                                      start=(f == 0), stop=(f == 31))
            p.inc(i, pe)

        @block.vector
        def _(vector):
            p.wait(vector, pe)
            p.wait(vector, ld, None)
            p.inc(vector.scalar_tensor_tensor(bcc[:], ps_all[:, 0:8], 0.0,
                                              b2cs[:], ALU.add, ALU.add), ve)

        if debug:
            @block.sync
            def _(sync):
                p.wait(sync, ve)
                for dt_ in range(DT):
                    p.inc(sync.dma_start(
                        out=dbg_wct.rearrange("(dt p) m -> p dt m", p=128)[:, dt_, :],
                        in_=wct[:, dt_, :]), so, 16)
                p.inc(sync.dma_start(out=dbg_bcc[:, :], in_=bcc[:]), so, 16)

        if debug == 2:
            @block.sync
            def _(sync):
                p.wait(sync, so)
            return p

        for tb in range(NTB):
            cur = xc[:, tb % 2]

            @block.scalar
            def _(scalar, tb=tb):
                p.wait(scalar, lx, 16 * (tb + 1))
                p.wait(scalar, pe)
                p.inc(scalar.activation(xbc[:], cur, AF.Copy), se)

            se_cast = p.val(se)

            @block.tensor
            def _(tensor, se_cast=se_cast):
                tensor.wait_ge(se, se_cast)
                p.wait(tensor, ve)
                for mt in range(DT):
                    for dt_ in range(DT):
                        i = tensor.matmul(bank(ps_all, mt),
                                          wct[:, dt_, mt * 128:(mt + 1) * 128],
                                          xbc[:, dt_, :],
                                          start=(dt_ == 0), stop=(dt_ == DT - 1))
                p.inc(i, pe)

            @block.vector
            def _(vector):
                p.wait(vector, pe)
                for mt in range(DT):
                    i = vector.scalar_tensor_tensor(
                        cur[:, mt, :], bank(ps_all, mt), bcc[:, mt:mt + 1],
                        cur[:, mt, :], ALU.add, ALU.add)
                p.inc(i, ve)

            emit_ln_chunk(p, block, lb, cur[:], cur[:], sems=sems)

            @block.sync
            def _(sync, tb=tb):
                p.wait(sync, ve)
                p.inc(sync.dma_start(out=xo[:, :, tb * 512:(tb + 1) * 512],
                                     in_=cur[:]), so, 16)
                if tb + 2 < NTB:
                    p.wait(sync, so)
                    p.inc(sync.dma_start(
                        out=xc[:, tb % 2],
                        in_=xv[:, :, (tb + 2) * 512:(tb + 3) * 512]), lx, 16)

        @block.sync
        def _(sync):
            p.wait(sync, so)
    return p


def l2_ffn_in_maps(inputs, xs):
    maps = []
    for c in range(8):
        w1 = _f32c(inputs["L2_ffn_W1"][c])
        w2t = _f32c(np.ascontiguousarray(np.asarray(inputs["L2_ffn_W2"][c]).T))
        b1c = _col(inputs["L2_ffn_b1"][c])
        b2c = _col(inputs["L2_ffn_b2"][c])
        maps.append({"x_in": xs[c], "w1": w1, "w2t": w2t, "b1c": b1c,
                     "b2c": b2c})
    return maps



def build_l0_ffn():
    p = Prog()
    nc = p.nc
    x_in = nc.dram_tensor("x_in", [D, T], F32, kind="ExternalInput").ap()
    wgt_h = nc.dram_tensor("wgt", [D, 3072], I32, kind="ExternalInput").ap()
    wut_h = nc.dram_tensor("wut", [D, 3072], I32, kind="ExternalInput").ap()
    wdt_h = nc.dram_tensor("wdt", [3072, D], I32, kind="ExternalInput").ap()
    x_out = nc.dram_tensor("x_out", [D, T], F32, kind="ExternalOutput").ap()

    FT = 24
    NR = 6
    xv = x_in.rearrange("(dt p) t -> p dt t", p=128)
    xo = x_out.rearrange("(dt p) t -> p dt t", p=128)
    R127 = 1.0 / 127.0

    with p.es:
        wgtb = p.sb("wgtb", [128, DT, 3072], BF16)
        wutb = p.sb("wutb", [128, DT, 3072], BF16)
        wdr = p.sb("wdr", [128, NR, D], BF16)
        h = p.sb("h", [128, FT, 512], BF16)
        sg = p.sb("sg", [128, 512], F32)
        r127 = p.sb("r127", [128, 1], F32)
        xc = p.sb("xc", [128, 2, DT, 512], F32)
        xbc = p.sb("xbc", [128, DT, 512], BF16)
        ps_all = p.psum("ps_all", [128, 4096], F32)
        lb = LnBufs(p, ps_all)

        ld = p.sem("ld")
        lwd = p.sem("lwd")
        lx = p.sem("lx")
        pe = p.sem("pe")
        se = p.sem("se")
        ve = p.sem("ve")
        so = p.sem("so")
        sems = {"pe": pe, "se": se, "ve": ve}

        block = p.es.enter_context(nc.Block())

        @block.sync
        def _(sync):
            for tb in range(2):
                p.inc(sync.dma_start(out=xc[:, tb],
                                     in_=xv[:, :, tb * 512:(tb + 1) * 512]), lx, 16)

        @block.gpsimd
        def _(g):
            for dt_ in range(DT):
                p.inc(g.dma_start(out=wgtb[:, dt_, :],
                                  in_=wgt_h[dt_ * 128:(dt_ + 1) * 128, :]), ld, 16)
            for dt_ in range(DT):
                p.inc(g.dma_start(out=wutb[:, dt_, :],
                                  in_=wut_h[dt_ * 128:(dt_ + 1) * 128, :]), ld, 16)

        ld_all = p.val(ld)
        emit_consts(p, block, lb, ve)

        @block.vector
        def _(vector):
            p.inc(vector.memset(r127[:], R127), ve)

        wd_pe_marks = {}

        for tb in range(NTB):
            cur = xc[:, tb % 2]

            @block.scalar
            def _(scalar, tb=tb):
                p.wait(scalar, lx, 16 * (tb + 1))
                p.wait(scalar, pe)
                p.inc(scalar.activation(xbc[:], cur, AF.Copy), se)

            se_cast = p.val(se)
            pe0 = p.val(pe)
            ve0 = p.val(ve)

            @block.tensor
            def _(tensor, se_cast=se_cast, ve0=ve0):
                p.wait(tensor, ld, ld_all)
                tensor.wait_ge(se, se_cast)
                for ft in range(FT):
                    gb = bank(ps_all, (ft % 2) * 2)
                    ub = bank(ps_all, (ft % 2) * 2 + 1)
                    if ft >= 2:
                        tensor.wait_ge(ve, ve0 + (ft - 1))
                    for dt_ in range(DT):
                        tensor.matmul(gb, wgtb[:, dt_, ft * 128:(ft + 1) * 128],
                                      xbc[:, dt_, :],
                                      start=(dt_ == 0), stop=(dt_ == DT - 1))
                    for dt_ in range(DT):
                        i = tensor.matmul(ub, wutb[:, dt_, ft * 128:(ft + 1) * 128],
                                          xbc[:, dt_, :],
                                          start=(dt_ == 0), stop=(dt_ == DT - 1))
                    i.then_inc(pe, 1)
            p.bump(pe, FT)

            @block.scalar
            def _(scalar, pe0=pe0, ve0=ve0):
                for ft in range(FT):
                    scalar.wait_ge(pe, pe0 + ft + 1)
                    if ft >= 1:
                        scalar.wait_ge(ve, ve0 + ft)
                    scalar.activation(sg[:], bank(ps_all, (ft % 2) * 2), AF.Silu,
                                      scale=r127[:, 0:1]).then_inc(se, 1)
            p.bump(se, FT)

            @block.vector
            def _(vector, se_cast=se_cast):
                for ft in range(FT):
                    vector.wait_ge(se, se_cast + ft + 1)
                    vector.scalar_tensor_tensor(
                        h[:, ft, :], sg[:], R127,
                        bank(ps_all, (ft % 2) * 2 + 1),
                        ALU.mult, ALU.mult).then_inc(ve, 1)
            p.bump(ve, FT)

            lwd0 = p.val(lwd)
            wd_pe_marks[tb] = p.val(pe)

            @block.gpsimd
            def _(g, tb=tb):
                for ft in range(FT):
                    gi = tb * FT + ft
                    if gi >= NR:
                        done_tb, done_ft = divmod(gi - NR, FT)
                        g.wait_ge(pe, wd_pe_marks[done_tb] + done_ft + 1)
                    g.dma_start(out=wdr[:, gi % NR, :],
                                in_=wdt_h[ft * 128:(ft + 1) * 128, :]).then_inc(lwd, 16)
            p.bump(lwd, FT * 16)

            ve_h = p.val(ve)

            @block.tensor
            def _(tensor, lwd0=lwd0, ve_h=ve_h, tb=tb):
                p.wait(tensor, ve, ve_h)
                for ft in range(FT):
                    tensor.wait_ge(lwd, lwd0 + 16 * (ft + 1))
                    for mt in range(DT):
                        i = tensor.matmul(bank(ps_all, mt),
                                          wdr[:, (tb * FT + ft) % NR, mt * 128:(mt + 1) * 128],
                                          h[:, ft, :],
                                          start=(ft == 0), stop=(ft == FT - 1))
                    i.then_inc(pe, 1)
            p.bump(pe, FT)

            @block.vector
            def _(vector):
                p.wait(vector, pe)
                for mt in range(DT):
                    i = vector.scalar_tensor_tensor(
                        cur[:, mt, :], bank(ps_all, mt), R127,
                        cur[:, mt, :], ALU.mult, ALU.add)
                p.inc(i, ve)

            emit_ln_chunk(p, block, lb, cur[:], cur[:], sems=sems)

            @block.sync
            def _(sync, tb=tb):
                p.wait(sync, ve)
                p.inc(sync.dma_start(out=xo[:, :, tb * 512:(tb + 1) * 512],
                                     in_=cur[:]), so, 16)
                if tb + 2 < NTB:
                    p.wait(sync, so)
                    p.inc(sync.dma_start(
                        out=xc[:, tb % 2],
                        in_=xv[:, :, (tb + 2) * 512:(tb + 3) * 512]), lx, 16)

        @block.sync
        def _(sync):
            p.wait(sync, so)
    return p


def l0_ffn_in_maps(inputs, xs):
    wgt = np.ascontiguousarray(np.asarray(inputs["L0_ffn_w_gate"]).T)
    wut = np.ascontiguousarray(np.asarray(inputs["L0_ffn_w_up"]).T)
    wdt = np.ascontiguousarray(np.asarray(inputs["L0_ffn_w_down"]).T)
    return [{"x_in": xs[c], "wgt": wgt, "wut": wut, "wdt": wdt}
            for c in range(8)]



def build_l3_ffn():
    p = Prog()
    nc = p.nc
    x_in = nc.dram_tensor("x_in", [D, T], F32, kind="ExternalInput").ap()
    ws1_h = nc.dram_tensor("ws1", [D, 512], F32, kind="ExternalInput").ap()
    ws2_h = nc.dram_tensor("ws2", [512, D], F32, kind="ExternalInput").ap()
    w1t_h = nc.dram_tensor("w1t", [D, 8, 512], F32, kind="ExternalInput").ap()
    w2r_h = nc.dram_tensor("w2r", [4096, D], F32, kind="ExternalInput").ap()
    wr_h = nc.dram_tensor("wr", [D, 8], F32, kind="ExternalInput").ap()
    eye_h = nc.dram_tensor("eyef", [128, 128], F32, kind="ExternalInput").ap()
    x_out = nc.dram_tensor("x_out", [D, T], F32, kind="ExternalOutput").ap()

    NR2 = 8
    xv = x_in.rearrange("(dt p) t -> p dt t", p=128)
    xo = x_out.rearrange("(dt p) t -> p dt t", p=128)

    with p.es:
        ws1b = p.sb("ws1b", [128, DT, 512], BF16)
        ws2b = p.sb("ws2b", [128, 4, D], BF16)
        w1r = p.sb("w1r", [128, 2, DT, 512], BF16)
        w2er = p.sb("w2er", [128, NR2, D], BF16)
        wrb = p.sb("wrb", [128, DT, 8], BF16)
        eyef = p.sb("eyef_s", [128, 128], F32)
        h = p.sb("h", [128, 36, 512], BF16)
        sg = p.sb("sg", [128, 2, 512], F32)
        sgm = p.sb("sgm", [128, 2, 512], F32)
        mrow = p.sb("mrow", [1, 8, 512], F32)
        rsc = p.sb("rsc", [128, 4, 8], F32)
        r1c = p.sb("r1c", [128, 4], F32)
        r2c = p.sb("r2c", [128, 4], F32)
        gec = p.sb("gec", [128, 4], F32)
        mfull = p.sb("mfull", [128, 4, 8], F32)
        xc = p.sb("xc", [128, 2, DT, 512], F32)
        xbc = p.sb("xbc", [128, DT, 512], BF16)
        ps_all = p.psum("ps_all", [128, 4096], F32)
        lb = LnBufs(p, ps_all)

        ld = p.sem("ld")
        lg = p.sem("lg")
        lw1 = p.sem("lw1")
        lw2 = p.sem("lw2")
        lx = p.sem("lx")
        pe = p.sem("pe")
        se = p.sem("se")
        ve = p.sem("ve")
        so = p.sem("so")
        sems = {"pe": pe, "se": se, "ve": ve}

        block = p.es.enter_context(nc.Block())

        @block.sync
        def _(sync):
            for tb in range(2):
                p.inc(sync.dma_start(out=xc[:, tb],
                                     in_=xv[:, :, tb * 512:(tb + 1) * 512]), lx, 16)
            p.inc(sync.dma_start(out=eyef[:], in_=eye_h[:, :]), ld, 16)

        @block.gpsimd
        def _(g):
            for dt_ in range(DT):
                p.inc(g.dma_start(out=ws1b[:, dt_, :],
                                  in_=ws1_h[dt_ * 128:(dt_ + 1) * 128, :]), lg, 16)
            for k in range(4):
                p.inc(g.dma_start(out=ws2b[:, k, :],
                                  in_=ws2_h[k * 128:(k + 1) * 128, :]), lg, 16)
            for dt_ in range(DT):
                p.inc(g.dma_start(out=wrb[:, dt_, :],
                                  in_=wr_h[dt_ * 128:(dt_ + 1) * 128, :]), lg, 16)
        ld_all = p.val(ld)
        lg_all = p.val(lg)

        emit_consts(p, block, lb, ve)

        w1_marks = {}
        w2_marks = {}

        for tb in range(NTB):
            cur = xc[:, tb % 2]

            @block.scalar
            def _(scalar, tb=tb):
                p.wait(scalar, lx, 16 * (tb + 1))
                p.wait(scalar, pe)
                p.inc(scalar.activation(xbc[:], cur, AF.Copy), se)

            se_cast = p.val(se)

            @block.tensor
            def _(tensor, se_cast=se_cast):
                p.wait(tensor, ld, ld_all)
                p.wait(tensor, lg, lg_all)
                tensor.wait_ge(se, se_cast)
                p.wait(tensor, ve)
                for st in range(4):
                    for dt_ in range(DT):
                        i = tensor.matmul(bank(ps_all, 6)[:, st * 8:(st + 1) * 8],
                                          xbc[:, dt_, st * 128:(st + 1) * 128],
                                          wrb[:, dt_, :],
                                          start=(dt_ == 0), stop=(dt_ == DT - 1))
                p.inc(i, pe)

            @block.vector
            def _(vector):
                p.wait(vector, pe)
                R = bank(ps_all, 6)[:, 0:32].rearrange("p (s e) -> p s e", e=8)
                p.inc(vector.tensor_reduce(r1c[:].unsqueeze(2), R, AX.X, ALU.max), ve)
                p.wait(vector, ve)
                r1b = r1c[:].unsqueeze(2).broadcast_to([128, 4, 8])
                p.inc(vector.tensor_tensor(rsc[:], R, r1b, ALU.is_equal), ve)
                p.wait(vector, ve)
                p.inc(vector.scalar_tensor_tensor(rsc[:], rsc[:], -1e30, R,
                                                  ALU.mult, ALU.add), ve)
                p.wait(vector, ve)
                p.inc(vector.tensor_reduce(r2c[:].unsqueeze(2), rsc[:], AX.X, ALU.max), ve)
                p.wait(vector, ve)
                p.inc(vector.tensor_sub(gec[:], r2c[:], r1c[:]), ve)
                p.inc(vector.tensor_tensor(mfull[:], R, r1b, ALU.is_equal), ve)

            @block.scalar
            def _(scalar):
                p.wait(scalar, ve)
                p.inc(scalar.activation(gec[:], gec[:], AF.Exp), se)

            @block.vector
            def _(vector):
                p.wait(vector, se)
                p.inc(vector.tensor_single_scalar(gec[:], gec[:], 1.0, ALU.add), ve)
                p.wait(vector, ve)
                p.inc(vector.reciprocal(gec[:], gec[:]), ve)
                p.wait(vector, ve)
                p.inc(vector.tensor_single_scalar(gec[:], gec[:], 1.0, ALU.is_ge), ve)
                p.wait(vector, ve)
                geb = gec[:].unsqueeze(2).broadcast_to([128, 4, 8])
                p.inc(vector.tensor_tensor(mfull[:], mfull[:], geb, ALU.mult), ve)

            for e in range(8):
                @block.tensor
                def _(tensor, e=e):
                    p.wait(tensor, ve)
                    for st in range(4):
                        i = tensor.transpose(bank(ps_all, 7)[0:1, st * 128:(st + 1) * 128],
                                             mfull[:, st, e:e + 1], eyef[:, :])
                    p.inc(i, pe)

                @block.vector
                def _(vector, e=e):
                    p.wait(vector, pe)
                    p.inc(vector.tensor_copy(mrow[0:1, e, :], bank(ps_all, 7)[0:1, 0:512]), ve)

            lw1_0 = p.val(lw1)

            pe1 = p.val(pe)
            se1 = p.val(se)
            ve1 = p.val(ve)
            w1_marks[tb] = pe1

            def V(ft, ve1=ve1):
                return ve1 + ft + 1

            def M(gf, ve1=ve1):
                return ve1 + 4 + 2 * (gf - 4) + 1

            def Q(gf, ve1=ve1):
                return M(gf, ve1) + 1

            @block.gpsimd
            def _(g, tb=tb):
                for e in range(8):
                    gi = tb * 8 + e
                    if gi >= 2:
                        done_tb, done_e = divmod(gi - 2, 8)
                        g.wait_ge(pe, w1_marks[done_tb] + 5 * done_e + 9)
                    for dt_ in range(DT):
                        g.dma_start(out=w1r[:, gi % 2, dt_, :],
                                    in_=w1t_h[dt_ * 128:(dt_ + 1) * 128, e, :]).then_inc(lw1, 16)
            p.bump(lw1, 8 * DT * 16)

            def PB(e, pe1=pe1):
                return pe1 + 4 + 5 * e + 1

            def P1(gf, pe1=pe1):
                if gf < 4:
                    return pe1 + gf + 1
                e, ft = divmod(gf - 4, 4)
                return PB(e) + ft + 1

            @block.tensor
            def _(tensor, lw1_0=lw1_0, se1=se1, tb=tb):
                for gf in range(36):
                    e, ft = divmod(gf - 4, 4)
                    if gf >= 4 and ft == 0:
                        if e >= 2:
                            tensor.wait_ge(ve, Q(4 + (e - 2) * 4 + 3))
                        tensor.matmul(bank(ps_all, 2 + e % 2),
                                      lb.ones_row_f32[0:1, :], mrow[0:1, e, :],
                                      start=True, stop=True).then_inc(pe, 1)
                        tensor.wait_ge(lw1, lw1_0 + 16 * DT * (e + 1))
                    if gf >= 2:
                        tensor.wait_ge(se, se1 + gf - 1)
                    bk = bank(ps_all, gf % 2)
                    for dt_ in range(DT):
                        if gf < 4:
                            wsl = ws1b[:, dt_, gf * 128:(gf + 1) * 128]
                        else:
                            wsl = w1r[:, e % 2, dt_, ft * 128:(ft + 1) * 128]
                        i = tensor.matmul(bk, wsl, xbc[:, dt_, :],
                                          start=(dt_ == 0), stop=(dt_ == DT - 1))
                    i.then_inc(pe, 1)
            p.bump(pe, 44)

            @block.scalar
            def _(scalar):
                for gf in range(36):
                    scalar.wait_ge(pe, P1(gf))
                    if gf >= 2:
                        scalar.wait_ge(ve, V(gf - 2) if gf - 2 < 4 else M(gf - 2))
                    scalar.activation(sg[:, gf % 2, :], bank(ps_all, gf % 2),
                                      AF.Relu).then_inc(se, 1)
            p.bump(se, 36)

            @block.vector
            def _(vector, se1=se1):
                for gf in range(36):
                    vector.wait_ge(se, se1 + gf + 1)
                    if gf < 4:
                        vector.tensor_mul(h[:, gf, :], sg[:, gf % 2, :],
                                          sg[:, gf % 2, :]).then_inc(ve, 1)
                    else:
                        e, ft = divmod(gf - 4, 4)
                        if ft == 0:
                            vector.wait_ge(pe, PB(e))
                        if gf >= 6:
                            vector.wait_ge(ve, Q(gf - 2))
                        vector.tensor_tensor(sgm[:, gf % 2, :], sg[:, gf % 2, :],
                                             bank(ps_all, 2 + e % 2), ALU.mult).then_inc(ve, 1)
                        vector.wait_ge(ve, M(gf))
                        vector.tensor_mul(h[:, gf, :], sgm[:, gf % 2, :],
                                          sgm[:, gf % 2, :]).then_inc(ve, 1)
            p.bump(ve, 4 + 64)

            lw2_0 = p.val(lw2)
            w2_marks[tb] = p.val(pe)

            @block.gpsimd
            def _(g, tb=tb):
                for s2 in range(32):
                    gi = tb * 32 + s2
                    if gi >= NR2:
                        done_tb, done_s = divmod(gi - NR2, 32)
                        g.wait_ge(pe, w2_marks[done_tb] + done_s + 4 + 1)
                    g.dma_start(out=w2er[:, gi % NR2, :],
                                in_=w2r_h[s2 * 128:(s2 + 1) * 128, :]).then_inc(lw2, 16)
            p.bump(lw2, 32 * 16)

            ve_h = p.val(ve)

            @block.tensor
            def _(tensor, lw2_0=lw2_0, tb=tb, ve_h=ve_h):
                p.wait(tensor, ve, ve_h)
                for src in range(36):
                    if src < 4:
                        wt = ws2b[:, src, :]
                    else:
                        tensor.wait_ge(lw2, lw2_0 + 16 * (src - 4 + 1))
                        wt = w2er[:, (tb * 32 + src - 4) % NR2, :]
                    for mt in range(DT):
                        i = tensor.matmul(bank(ps_all, mt), wt[:, mt * 128:(mt + 1) * 128],
                                          h[:, src, :],
                                          start=(src == 0), stop=(src == 35))
                    i.then_inc(pe, 1)
            p.bump(pe, 36)

            @block.vector
            def _(vector):
                p.wait(vector, pe)
                for mt in range(DT):
                    i = vector.scalar_tensor_tensor(
                        cur[:, mt, :], bank(ps_all, mt), 0.0,
                        cur[:, mt, :], ALU.add, ALU.add)
                p.inc(i, ve)

            emit_ln_chunk(p, block, lb, cur[:], cur[:], sems=sems)

            @block.sync
            def _(sync, tb=tb):
                p.wait(sync, ve)
                p.inc(sync.dma_start(out=xo[:, :, tb * 512:(tb + 1) * 512],
                                     in_=cur[:]), so, 16)
                if tb + 2 < NTB:
                    p.wait(sync, so)
                    p.inc(sync.dma_start(
                        out=xc[:, tb % 2],
                        in_=xv[:, :, (tb + 2) * 512:(tb + 3) * 512]), lx, 16)

        @block.sync
        def _(sync):
            p.wait(sync, so)
    return p


def l3_ffn_in_maps(inputs, xs):
    ws1 = _f32c(inputs["L3_ffn_Ws1"])
    ws2 = _f32c(inputs["L3_ffn_Ws2"])
    w1t = np.ascontiguousarray(np.asarray(inputs["L3_ffn_W1"], np.float32).transpose(1, 0, 2))
    w2r = np.ascontiguousarray(np.asarray(inputs["L3_ffn_W2"], np.float32).reshape(4096, D))
    wr = _f32c(inputs["L3_ffn_Wr"])
    eye = np.eye(128, dtype=np.float32)
    return [{"x_in": xs[c], "ws1": ws1, "ws2": ws2, "w1t": w1t, "w2r": w2r,
             "wr": wr, "eyef": eye} for c in range(8)]



def build_l2_attn(debug=0):
    w_marks_l2 = {}
    p = Prog()
    nc = p.nc
    x_in = nc.dram_tensor("x_in", [D, T], F32, kind="ExternalInput").ap()
    wq_h = nc.dram_tensor("wq", [D, 2048], F32, kind="ExternalInput").ap()
    wk_h = nc.dram_tensor("wk", [D, 2048], F32, kind="ExternalInput").ap()
    wv_h = nc.dram_tensor("wv", [D, 2048], F32, kind="ExternalInput").ap()
    wo_h = nc.dram_tensor("wo", [2048, D], F32, kind="ExternalInput").ap()
    msk_h = nc.dram_tensor("cmask", [128, 1024], BF16, kind="ExternalInput").ap()
    eyb_h = nc.dram_tensor("eyeb", [128, 128], BF16, kind="ExternalInput").ap()
    x_out = nc.dram_tensor("x_out", [D, T], F32, kind="ExternalOutput").ap()
    if debug:
        dbg_ob = nc.dram_tensor("dbg_ob", [128, 16, T], BF16, kind="ExternalOutput").ap()

    xv = x_in.rearrange("(dt p) t -> p dt t", p=128)
    xo = x_out.rearrange("(dt p) t -> p dt t", p=128)

    with p.es:
        xb = p.sb("xb", [128, DT, T], BF16)
        obuf = p.sb("obuf", [128, 16, T], BF16)
        xc = p.sb("xc", [128, DT, 512], F32)
        wv4 = p.sb("wv4", [128, DT, 512], BF16)
        wqh = p.sb("wqh", [128, 2, DT, 128], BF16)
        wkh = p.sb("wkh", [128, 2, DT, 128], BF16)
        wor = p.sb("wor", [128, 1, 16, 128], BF16)
        v4 = p.sb("v4", [128, 16, 512], BF16)
        qs = p.sb("qs", [128, T], BF16)
        ks = p.sb("ks", [128, T], BF16)
        pt = p.sb("pt", [128, 16, 512], BF16)
        zrow = p.sb("zrow", [1, 512], F32)
        zbs = p.sb("zbs", [128, 512], F32)
        rows2 = p.sb("rows2", [1, 2, T], F32)
        nrow = p.sb("nrow", [1, T], BF16)
        mxk = p.sb("mxk", [1, 1], F32)
        cmask = p.sb("cmask_s", [128, 1024], BF16)
        eyeb = p.sb("eyeb_s", [128, 128], BF16)
        ones_row_bf = p.sb("ones_row_bf", [1, 128], BF16)
        r8 = p.sb("r8", [128, 1], F32)
        ch05 = p.sb("ch05", [1, 1], F32)
        ps_all = p.psum("ps_all", [128, 4096], F32)
        lb = LnBufs(p, ps_all, tb=pt[:, 0:8, :], sq=pt[:, 8:16, :],
                    rows=rows2[0:1, 0, :])

        ld = p.sem("ld")
        lg = p.sem("lg")
        lqk = p.sem("lqk")
        lwo = p.sem("lwo")
        lx = p.sem("lx")
        pe = p.sem("pe")
        se = p.sem("se")
        ve = p.sem("ve")
        so = p.sem("so")
        sems = {"pe": pe, "se": se, "ve": ve}

        block = p.es.enter_context(nc.Block())


        @block.gpsimd
        def _(g):
            p.inc(g.dma_start(out=cmask[:], in_=msk_h[:, :]), lg, 16)
            p.inc(g.dma_start(out=eyeb[:], in_=eyb_h[:, :]), lg, 16)

        emit_consts(p, block, lb, ve)

        @block.vector
        def _(vector):
            vector.memset(r8[:], 0.125)
            vector.memset(ch05[:], 0.5)
            p.inc(vector.memset(ones_row_bf[:], 1.0), ve)

        for tb in range(NTB):
            @block.sync
            def _(sync, tb=tb):
                p.wait(sync, se)
                p.inc(sync.dma_start(out=xc[:],
                                     in_=xv[:, :, tb * 512:(tb + 1) * 512]), lx, 16)

            @block.scalar
            def _(scalar, tb=tb):
                p.wait(scalar, lx, 16 * (tb + 1))
                p.inc(scalar.activation(xb[:, :, tb * 512:(tb + 1) * 512],
                                        xc[:], AF.Copy), se)

        se_cast_all = p.val(se)

        for hg in range(4):
            @block.gpsimd
            def _(g, hg=hg):
                p.wait(g, pe)
                for dt_ in range(DT):
                    g.dma_start(out=wv4[:, dt_, :],
                                in_=wv_h[dt_ * 128:(dt_ + 1) * 128,
                                         hg * 512:(hg + 1) * 512]).then_inc(lg, 16)
            p.bump(lg, DT * 16)
            lg_v = p.val(lg)
            pe_v0 = p.val(pe)
            se_v0 = p.val(se)

            @block.tensor
            def _(tensor, lg_v=lg_v, se_v0=se_v0):
                tensor.wait_ge(lg, lg_v)
                tensor.wait_ge(se, se_cast_all)
                p.wait(tensor, ve)
                for tt in range(16):
                    if tt >= 2:
                        tensor.wait_ge(se, se_v0 + (tt - 1))
                    bk = bank(ps_all, 6 + tt % 2)
                    for dt_ in range(DT):
                        i = tensor.matmul(bk, xb[:, dt_, tt * 128:(tt + 1) * 128],
                                          wv4[:, dt_, :],
                                          start=(dt_ == 0), stop=(dt_ == DT - 1))
                    i.then_inc(pe, 1)
            p.bump(pe, 16)

            @block.scalar
            def _(scalar, pe_v0=pe_v0):
                for tt in range(16):
                    scalar.wait_ge(pe, pe_v0 + tt + 1)
                    scalar.activation(v4[:, tt, :], bank(ps_all, 6 + tt % 2),
                                      AF.Copy).then_inc(se, 1)
            p.bump(se, 16)

            for hl in range(4):
                h = hg * 4 + hl

                @block.gpsimd
                def _(g, h=h):
                    p.wait(g, pe)
                    for dt_ in range(DT):
                        g.dma_start(out=wqh[:, h % 2, dt_, :],
                                    in_=wq_h[dt_ * 128:(dt_ + 1) * 128,
                                             h * 128:(h + 1) * 128]).then_inc(lqk, 16)
                    for dt_ in range(DT):
                        g.dma_start(out=wkh[:, h % 2, dt_, :],
                                    in_=wk_h[dt_ * 128:(dt_ + 1) * 128,
                                             h * 128:(h + 1) * 128]).then_inc(lqk, 16)
                p.bump(lqk, 2 * DT * 16)
                lqk_h = p.val(lqk)

                pe_q0 = p.val(pe)
                se_q0 = p.val(se)

                @block.tensor
                def _(tensor, h=h, lqk_h=lqk_h, se_q0=se_q0):
                    tensor.wait_ge(lqk, lqk_h)
                    p.wait(tensor, ve)
                    for w in range(8):
                        isq, tj = (w < 4), w % 4
                        if w >= 2:
                            tensor.wait_ge(se, se_q0 + 2 * (w - 1))
                        bk = bank(ps_all, w % 2)
                        wsl = wqh if isq else wkh
                        for dt_ in range(DT):
                            i = tensor.matmul(bk, wsl[:, h % 2, dt_, :],
                                              xb[:, dt_, tj * 512:(tj + 1) * 512],
                                              start=(dt_ == 0), stop=(dt_ == DT - 1))
                        i.then_inc(pe, 1)
                p.bump(pe, 8)

                @block.scalar
                def _(scalar, pe_q0=pe_q0):
                    for w in range(8):
                        isq, tj = (w < 4), w % 4
                        scalar.wait_ge(pe, pe_q0 + w + 1)
                        bk = bank(ps_all, w % 2)
                        dst = qs if isq else ks
                        if isq:
                            scalar.activation(dst[:, tj * 512:(tj + 1) * 512], bk,
                                              AF.Copy, scale=r8[:, 0:1]).then_inc(se, 1)
                            scalar.activation(lb.sq[:, w, :], bk, AF.Square,
                                              scale=r8[:, 0:1]).then_inc(se, 1)
                        else:
                            scalar.activation(dst[:, tj * 512:(tj + 1) * 512], bk,
                                              AF.Copy).then_inc(se, 1)
                            scalar.activation(lb.sq[:, w, :], bk,
                                              AF.Square).then_inc(se, 1)
                p.bump(se, 16)

                pe_r0 = p.val(pe)
                ve_r0 = p.val(ve)

                @block.tensor
                def _(tensor, se_q0=se_q0, ve_r0=ve_r0):
                    for w in range(8):
                        tensor.wait_ge(se, se_q0 + 2 * w + 2)
                        if w >= 2:
                            tensor.wait_ge(ve, ve_r0 + (w - 1))
                        i = tensor.matmul(bank(ps_all, 2 + w % 2)[0:1, :],
                                          lb.ones_col_bf[:, 0:1], lb.sq[:, w, :],
                                          start=True, stop=True)
                        i.then_inc(pe, 1)
                p.bump(pe, 8)

                @block.vector
                def _(vector, pe_r0=pe_r0):
                    for w in range(8):
                        vector.wait_ge(pe, pe_r0 + w + 1)
                        r, tj = (0 if w < 4 else 1), w % 4
                        vector.tensor_copy(rows2[0:1, r, tj * 512:(tj + 1) * 512],
                                           bank(ps_all, 2 + w % 2)[0:1, :]).then_inc(ve, 1)
                p.bump(ve, 8)

                @block.scalar
                def _(scalar):
                    p.wait(scalar, ve)
                    p.inc(scalar.activation(rows2[0:1, 0, :], rows2[0:1, 0, :], AF.Ln), se)
                    p.inc(scalar.activation(rows2[0:1, 1, :], rows2[0:1, 1, :], AF.Ln), se)
                    p.wait(scalar, se)
                    p.inc(scalar.activation(rows2[0:1, 0, :], rows2[0:1, 0, :],
                                            AF.Exp, scale=ch05[0:1, 0:1]), se)
                    p.inc(scalar.activation(rows2[0:1, 1, :], rows2[0:1, 1, :],
                                            AF.Exp, scale=ch05[0:1, 0:1]), se)

                @block.vector
                def _(vector):
                    p.wait(vector, se)
                    p.inc(vector.tensor_reduce(mxk[0:1, 0:1], rows2[0:1, 1, :],
                                               AX.X, ALU.max), ve)
                    p.wait(vector, ve)
                    p.inc(vector.tensor_scalar(nrow[0:1, :], rows2[0:1, 0, :],
                                               mxk[0:1, 0:1], -1.0,
                                               ALU.mult, ALU.mult), ve)

                for j in range(NTB):
                    K = 4 * (j + 1)
                    pe_s0 = p.val(pe)
                    se_s0 = p.val(se)

                    def S_inc(tk, pe_s0=pe_s0):
                        return pe_s0 + (1 if tk == 0 else 2 * tk)

                    def Z_inc(tk, K=K, pe_s0=pe_s0):
                        if tk <= K - 3:
                            return pe_s0 + 2 * tk + 3
                        return pe_s0 + 2 * (K - 1) + (tk - (K - 3))

                    @block.tensor
                    def _(tensor, h=h, hl=hl, j=j, K=K, se_s0=se_s0):
                        p.wait(tensor, ve)
                        for tk in range(K):
                            if tk >= 2:
                                tensor.wait_ge(se, se_s0 + (tk - 1))
                            bk = bank(ps_all, tk % 2)
                            tensor.matmul(bk, ks[:, tk * 128:(tk + 1) * 128],
                                          qs[:, j * 512:(j + 1) * 512],
                                          start=True, stop=False)
                            diag = (tk >= 4 * j)
                            i = tensor.matmul(bk, ones_row_bf[0:1, :],
                                              nrow[0:1, j * 512:(j + 1) * 512],
                                              start=False, stop=not diag)
                            if diag:
                                o = tk * 128 - j * 512
                                i = tensor.matmul(bk, eyeb[:, :],
                                                  cmask[:, 384 - o:384 - o + 512],
                                                  start=False, stop=True)
                            i.then_inc(pe, 1)
                            if tk >= 1:
                                tkp = tk - 1
                                tensor.wait_ge(se, se_s0 + tkp + 1)
                                tensor.matmul(bank(ps_all, 2),
                                              v4[:, tkp, hl * 128:(hl + 1) * 128],
                                              pt[:, tkp, :],
                                              start=(tkp == 0), stop=(tkp == K - 1))
                                tensor.matmul(bank(ps_all, 3)[0:1, :],
                                              lb.ones_col_bf[:, 0:1], pt[:, tkp, :],
                                              start=(tkp == 0),
                                              stop=(tkp == K - 1)).then_inc(pe, 1)
                        tkp = K - 1
                        tensor.wait_ge(se, se_s0 + tkp + 1)
                        tensor.matmul(bank(ps_all, 2),
                                      v4[:, tkp, hl * 128:(hl + 1) * 128],
                                      pt[:, tkp, :],
                                      start=(tkp == 0), stop=True)
                        tensor.matmul(bank(ps_all, 3)[0:1, :],
                                      lb.ones_col_bf[:, 0:1], pt[:, tkp, :],
                                      start=(tkp == 0), stop=True).then_inc(pe, 1)
                    p.bump(pe, 2 * K)

                    @block.scalar
                    def _(scalar, j=j, K=K, pe_s0=pe_s0):
                        p.wait(scalar, pe, pe_s0)
                        for tk in range(K):
                            scalar.wait_ge(pe, S_inc(tk))
                            scalar.activation(pt[:, tk, :], bank(ps_all, tk % 2),
                                              AF.Exp).then_inc(se, 1)
                    p.bump(se, K)

                    @block.vector
                    def _(vector, K=K):
                        vector.wait_ge(pe, Z_inc(K - 1))
                        p.inc(vector.reciprocal(zrow[0:1, :], bank(ps_all, 3)[0:1, :]), ve)

                    @block.tensor
                    def _(tensor):
                        p.wait(tensor, ve)
                        p.inc(tensor.matmul(bank(ps_all, 4), lb.ones_row_f32[0:1, :],
                                            zrow[0:1, :], start=True, stop=True), pe)

                    @block.vector
                    def _(vector, h=h, j=j):
                        p.wait(vector, pe)
                        p.inc(vector.tensor_copy(zbs[:], bank(ps_all, 4)), ve)
                        p.wait(vector, ve)
                        p.inc(vector.tensor_tensor(obuf[:, h, j * 512:(j + 1) * 512],
                                                   bank(ps_all, 2), zbs[:],
                                                   ALU.mult), ve)

        if debug:
            @block.sync
            def _(sync):
                p.wait(sync, ve)
                p.inc(sync.dma_start(out=dbg_ob[:, :, :], in_=obuf[:]), so, 16)

        lx_c0 = p.val(lx)

        for j in range(NTB):
            @block.sync
            def _(sync, j=j):
                p.wait(sync, so)
                sync.wait_ge(se, se_cast_all)
                p.inc(sync.dma_start(out=xc[:],
                                     in_=xv[:, :, j * 512:(j + 1) * 512]), lx, 16)

            for mt in range(DT):
                gi = j * DT + mt

                @block.gpsimd
                def _(g, j=j, mt=mt, gi=gi):
                    if gi >= 1:
                        g.wait_ge(pe, w_marks_l2[gi - 1])
                    g.dma_start(out=wor[:, 0],
                                in_=wo_h[:, mt * 128:(mt + 1) * 128]
                                .rearrange("(ht p) m -> p ht m", p=128)).then_inc(lwo, 16)
                p.bump(lwo, 16)

                @block.tensor
                def _(tensor, j=j, mt=mt, gi=gi):
                    if mt == 0:
                        p.wait(tensor, ve)
                        p.wait(tensor, lx, lx_c0 + 16 * (j + 1))
                    tensor.wait_ge(lwo, 16 * (gi + 1))
                    if gi >= 2:
                        p.wait(tensor, ve)
                    bk = bank(ps_all, mt % 2)
                    for hh in range(16):
                        i = tensor.matmul(bk, wor[:, 0, hh, :],
                                          obuf[:, hh, j * 512:(j + 1) * 512],
                                          start=(hh == 0), stop=(hh == 15))
                    i.then_inc(pe, 1)
                p.bump(pe, 1)
                w_marks_l2[gi] = p.val(pe)

                @block.vector
                def _(vector, j=j, mt=mt):
                    p.wait(vector, pe)
                    p.inc(vector.scalar_tensor_tensor(
                        xc[:, mt, :], bank(ps_all, mt % 2), 0.0,
                        xc[:, mt, :], ALU.add, ALU.add), ve)

            emit_ln_chunk(p, block, lb, xc[:], xc[:], sems=sems)

            @block.sync
            def _(sync, j=j):
                p.wait(sync, ve)
                p.inc(sync.dma_start(out=xo[:, :, j * 512:(j + 1) * 512],
                                     in_=xc[:]), so, 16)

        @block.sync
        def _(sync):
            p.wait(sync, so)
    return p


def l2_attn_in_maps(inputs, xs):
    wq = _f32c(inputs["L2_attn_Wq"])
    wk = _f32c(inputs["L2_attn_Wk"])
    wv = _f32c(inputs["L2_attn_Wv"])
    wo = _f32c(inputs["L2_attn_Wo"])
    ii = np.arange(128)[:, None]
    cc = np.arange(1024)[None, :]
    cm = np.where(cc < ii + 384, np.float32(-1e9), np.float32(0.0)).astype(BF)
    eyb = np.eye(128, dtype=BF)
    return [{"x_in": xs[c], "wq": wq, "wk": wk, "wv": wv, "wo": wo,
             "cmask": cm, "eyeb": eyb} for c in range(8)]



def build_l3_attn():
    p = Prog()
    nc = p.nc
    x_in = nc.dram_tensor("x_in", [D, T], F32, kind="ExternalInput").ap()
    wqt_h = nc.dram_tensor("wqt", [D, D], F32, kind="ExternalInput").ap()
    wkt_h = nc.dram_tensor("wkt", [D, D], F32, kind="ExternalInput").ap()
    wvt_h = nc.dram_tensor("wvt", [D, D], F32, kind="ExternalInput").ap()
    wot_h = nc.dram_tensor("wot", [D, D], F32, kind="ExternalInput").ap()
    bqc_h = nc.dram_tensor("bqc", [128, 8], F32, kind="ExternalInput").ap()
    bkc_h = nc.dram_tensor("bkc", [128, 8], F32, kind="ExternalInput").ap()
    bvr_h = nc.dram_tensor("bvr", [1, D], F32, kind="ExternalInput").ap()
    boc_h = nc.dram_tensor("boc", [128, 8], F32, kind="ExternalInput").ap()
    cos_h = nc.dram_tensor("cosb", [128, T], F32, kind="ExternalInput").ap()
    sin_h = nc.dram_tensor("sinb", [128, T], F32, kind="ExternalInput").ap()
    r2t_h = nc.dram_tensor("r2t", [128, 128], BF16, kind="ExternalInput").ap()
    x_out = nc.dram_tensor("x_out", [D, T], F32, kind="ExternalOutput").ap()

    xv = x_in.rearrange("(dt p) t -> p dt t", p=128)
    xo = x_out.rearrange("(dt p) t -> p dt t", p=128)

    with p.es:
        xb = p.sb("xb", [128, DT, T], BF16)
        obuf = p.sb("obuf", [128, DT, T], BF16)
        xc = p.sb("xc", [128, DT, 512], F32)
        wqp = p.sb("wqp", [128, 2, DT, 128], BF16)
        wkp = p.sb("wkp", [128, 2, DT, 128], BF16)
        wvp = p.sb("wvp", [128, 2, DT, 128], BF16)
        wor = p.sb("wor", [128, DT, 128], BF16)
        v2 = p.sb("v2", [128, 16, 128], BF16)
        qs = p.sb("qs", [128, T], BF16)
        ks = p.sb("ks", [128, T], BF16)
        qf = p.sb("qf", [128, 2, 512], F32)
        qb2 = p.sb("qb2", [128, 2, 512], BF16)
        rsc = p.sb("rsc", [128, 4, 512], F32)
        pt2 = p.sb("pt2", [128, 32, 512], BF16)
        zrow2 = p.sb("zrow2", [1, 2, 512], F32)
        rows4 = p.sb("rows4", [1, 4, T], BF16)
        nrow2 = p.sb("nrow2", [1, 2, T], BF16)
        mxk2 = p.sb("mxk2", [1, 2], F32)
        cosb = p.sb("cosb_s", [128, T], F32)
        sinb = p.sb("sinb_s", [128, T], F32)
        r2t = p.sb("r2t_s", [128, 128], BF16)
        bqc8 = p.sb("bqc8", [128, 8], F32)
        bqc = p.sb("bqc_s", [128, 8], F32)
        bkc = p.sb("bkc_s", [128, 8], F32)
        bvr = p.sb("bvr_s", [1, D], BF16)
        boc = p.sb("boc_s", [128, 8], F32)
        ones_row_bf = p.sb("ones_row_bf", [1, 128], BF16)
        r8 = p.sb("r8", [128, 1], F32)
        ch05 = p.sb("ch05", [1, 1], F32)
        ps_all = p.psum("ps_all", [128, 4096], F32)
        lb = LnBufs(p, ps_all, tb=pt2[:, 0:8, :], sq=pt2[:, 8:16, :],
                    scr=rsc[:, 0:2, :])

        ld = p.sem("ld")
        lg = p.sem("lg")
        lqk = p.sem("lqk")
        lwo = p.sem("lwo")
        lx = p.sem("lx")
        pe = p.sem("pe")
        se = p.sem("se")
        ve = p.sem("ve")
        so = p.sem("so")
        sems = {"pe": pe, "se": se, "ve": ve}

        block = p.es.enter_context(nc.Block())

        @block.sync
        def _(sync):
            p.inc(sync.dma_start(out=cosb[:], in_=cos_h[:, :]), ld, 16)
            p.inc(sync.dma_start(out=sinb[:], in_=sin_h[:, :]), ld, 16)
            p.inc(sync.dma_start(out=bqc[:], in_=bqc_h[:, :]), ld, 16)
            p.inc(sync.dma_start(out=bkc[:], in_=bkc_h[:, :]), ld, 16)
            p.inc(sync.dma_start(out=boc[:], in_=boc_h[:, :]), ld, 16)

        @block.gpsimd
        def _(g):
            p.inc(g.dma_start(out=r2t[:], in_=r2t_h[:, :]), lg, 16)
            p.inc(g.dma_start(out=bvr[:], in_=bvr_h[:, :]), lg, 16)

        emit_consts(p, block, lb, ve)

        @block.vector
        def _(vector):
            vector.memset(r8[:], 0.125)
            vector.memset(ch05[:], 0.5)
            p.inc(vector.memset(ones_row_bf[:], 1.0), ve)
            p.wait(vector, ld, 5 * 16)
            p.inc(vector.tensor_single_scalar(bqc8[:], bqc[:], 0.125, ALU.mult), ve)

        for tb in range(NTB):
            @block.sync
            def _(sync, tb=tb):
                p.wait(sync, se)
                p.inc(sync.dma_start(out=xc[:],
                                     in_=xv[:, :, tb * 512:(tb + 1) * 512]), lx, 16)

            @block.scalar
            def _(scalar, tb=tb):
                p.wait(scalar, lx, 16 * (tb + 1))
                p.inc(scalar.activation(xb[:, :, tb * 512:(tb + 1) * 512],
                                        xc[:], AF.Copy), se)

        se_cast_all = p.val(se)
        ld_all = p.val(ld)
        lg_all = p.val(lg)

        for hp in range(DT):
            @block.gpsimd
            def _(g, hp=hp):
                p.wait(g, pe)
                for wsl, src in ((wqp, wqt_h), (wkp, wkt_h), (wvp, wvt_h)):
                    for dt_ in range(DT):
                        g.dma_start(out=wsl[:, hp % 2, dt_, :],
                                    in_=src[dt_ * 128:(dt_ + 1) * 128,
                                            hp * 128:(hp + 1) * 128]).then_inc(lqk, 16)
            p.bump(lqk, 3 * DT * 16)
            lqk_h = p.val(lqk)

            pe_v0 = p.val(pe)
            se_v0 = p.val(se)

            @block.tensor
            def _(tensor, hp=hp, lqk_h=lqk_h, se_v0=se_v0):
                tensor.wait_ge(lqk, lqk_h)
                tensor.wait_ge(se, se_cast_all)
                p.wait(tensor, ld, ld_all)
                p.wait(tensor, lg, lg_all)
                p.wait(tensor, ve)
                for tt in range(16):
                    if tt >= 2:
                        tensor.wait_ge(se, se_v0 + (tt - 1))
                    bk = bank(ps_all, 6 + tt % 2, 128)
                    for dt_ in range(DT):
                        tensor.matmul(bk, xb[:, dt_, tt * 128:(tt + 1) * 128],
                                      wvp[:, hp % 2, dt_, :],
                                      start=(dt_ == 0), stop=False)
                    i = tensor.matmul(bk, ones_row_bf[0:1, tt * 8:tt * 8 + 128]
                                      if False else ones_row_bf[0:1, :],
                                      bvr[0:1, hp * 128:(hp + 1) * 128],
                                      start=False, stop=True)
                    i.then_inc(pe, 1)
            p.bump(pe, 16)

            @block.scalar
            def _(scalar, pe_v0=pe_v0):
                for tt in range(16):
                    scalar.wait_ge(pe, pe_v0 + tt + 1)
                    scalar.activation(v2[:, tt, :], bank(ps_all, 6 + tt % 2, 128),
                                      AF.Copy).then_inc(se, 1)
            p.bump(se, 16)

            pe_q0 = p.val(pe)
            se_q0 = p.val(se)
            ve_q0 = p.val(ve)

            def proj_inc(w, pe_q0=pe_q0):
                return pe_q0 + (w + 1 if w < 3 else 2 * w - 1)

            def R_inc(w, pe_q0=pe_q0):
                return pe_q0 + (2 * w + 4 if w <= 5 else w + 9)

            @block.tensor
            def _(tensor, hp=hp, se_q0=se_q0, ve_q0=ve_q0):
                def emit_R(w):
                    tensor.wait_ge(se, se_q0 + w + 1)
                    if w >= 2:
                        tensor.wait_ge(ve, ve_q0 + 3 * (w - 2) + 1)
                    tensor.matmul(bank(ps_all, 2 + w % 2), r2t[:, :],
                                  qb2[:, w % 2, :], start=True,
                                  stop=True).then_inc(pe, 1)

                for w in range(8):
                    isq, tj = (w < 4), w % 4
                    if w >= 2:
                        tensor.wait_ge(se, se_q0 + w - 1)
                    bk = bank(ps_all, w % 2)
                    wsl = wqp if isq else wkp
                    for dt_ in range(DT):
                        i = tensor.matmul(bk, wsl[:, hp % 2, dt_, :],
                                          xb[:, dt_, tj * 512:(tj + 1) * 512],
                                          start=(dt_ == 0), stop=(dt_ == DT - 1))
                    i.then_inc(pe, 1)
                    if w >= 2:
                        emit_R(w - 2)
                for w in range(6, 8):
                    emit_R(w)
            p.bump(pe, 16)

            @block.scalar
            def _(scalar, hp=hp, ve_q0=ve_q0):
                for w in range(8):
                    isq, tj = (w < 4), w % 4
                    scalar.wait_ge(pe, proj_inc(w))
                    if w >= 2:
                        scalar.wait_ge(ve, ve_q0 + 3 * (w - 2) + 2)
                    bk = bank(ps_all, w % 2)
                    if isq:
                        scalar.activation(qf[:, w % 2, :], bk, AF.Identity,
                                          scale=r8[:, 0:1], bias=bqc8[:, hp:hp + 1])
                        scalar.activation(qb2[:, w % 2, :], bk, AF.Identity,
                                          scale=r8[:, 0:1], bias=bqc8[:, hp:hp + 1])
                        scalar.activation(lb.sq[:, w, :], bk, AF.Square,
                                          scale=r8[:, 0:1],
                                          bias=bqc8[:, hp:hp + 1]).then_inc(se, 1)
                    else:
                        scalar.activation(qf[:, w % 2, :], bk, AF.Identity,
                                          bias=bkc[:, hp:hp + 1])
                        scalar.activation(qb2[:, w % 2, :], bk, AF.Identity,
                                          bias=bkc[:, hp:hp + 1])
                        scalar.activation(lb.sq[:, w, :], bk, AF.Square,
                                          bias=bkc[:, hp:hp + 1]).then_inc(se, 1)
            p.bump(se, 8)

            @block.vector
            def _(vector, ve_q0=ve_q0):
                for w in range(8):
                    isq, tj = (w < 4), w % 4
                    vector.wait_ge(pe, R_inc(w))
                    dst = qs if isq else ks
                    cs = (tj * 512, (tj + 1) * 512)
                    vector.tensor_tensor(rsc[:, 2 * (w % 2), :],
                                         bank(ps_all, 2 + w % 2),
                                         sinb[:, cs[0]:cs[1]],
                                         ALU.mult).then_inc(ve, 1)
                    vector.tensor_tensor(rsc[:, 2 * (w % 2) + 1, :],
                                         qf[:, w % 2, :],
                                         cosb[:, cs[0]:cs[1]],
                                         ALU.mult).then_inc(ve, 1)
                    vector.wait_ge(ve, ve_q0 + 3 * w + 2)
                    vector.tensor_add(dst[:, cs[0]:cs[1]],
                                      rsc[:, 2 * (w % 2), :],
                                      rsc[:, 2 * (w % 2) + 1, :]).then_inc(ve, 1)
            p.bump(ve, 24)

            pe_w0 = p.val(pe)
            ve_w0 = p.val(ve)

            @block.tensor
            def _(tensor, se_q0=se_q0, ve_w0=ve_w0):
                for w in range(8):
                    for hh in range(2):
                        idx = 2 * w + hh
                        tensor.wait_ge(se, se_q0 + w + 1)
                        if idx < 2:
                            p.wait(tensor, ve)
                        else:
                            tensor.wait_ge(ve, ve_w0 + idx - 1)
                        i = tensor.matmul(
                            bank(ps_all, 4 + idx % 2)[0:1, :],
                            lb.ones_col_bf[hh * 64:(hh + 1) * 64, 0:1],
                            lb.sq[hh * 64:(hh + 1) * 64, w, :],
                            start=True, stop=True)
                        i.then_inc(pe, 1)
            p.bump(pe, 16)

            @block.vector
            def _(vector, pe_w0=pe_w0):
                for w in range(8):
                    for hh in range(2):
                        idx = 2 * w + hh
                        vector.wait_ge(pe, pe_w0 + idx + 1)
                        isq, tj = (w < 4), w % 4
                        r = (0 if isq else 2) + hh
                        vector.tensor_copy(rows4[0:1, r, tj * 512:(tj + 1) * 512],
                                           bank(ps_all, 4 + idx % 2)[0:1, :]).then_inc(ve, 1)
            p.bump(ve, 16)

            @block.scalar
            def _(scalar):
                p.wait(scalar, ve)
                for r in range(4):
                    p.inc(scalar.activation(rows4[0:1, r, :], rows4[0:1, r, :],
                                            AF.Ln), se)
                p.wait(scalar, se)
                for r in range(4):
                    p.inc(scalar.activation(rows4[0:1, r, :], rows4[0:1, r, :],
                                            AF.Exp, scale=ch05[0:1, 0:1]), se)

            @block.vector
            def _(vector):
                p.wait(vector, se)
                p.inc(vector.tensor_reduce(mxk2[0:1, 0:1], rows4[0:1, 2, :],
                                           AX.X, ALU.max), ve)
                p.inc(vector.tensor_reduce(mxk2[0:1, 1:2], rows4[0:1, 3, :],
                                           AX.X, ALU.max), ve)
                p.wait(vector, ve)
                p.inc(vector.tensor_scalar(nrow2[0:1, 0, :], rows4[0:1, 0, :],
                                           mxk2[0:1, 0:1], -1.0,
                                           ALU.mult, ALU.mult), ve)
                p.inc(vector.tensor_scalar(nrow2[0:1, 1, :], rows4[0:1, 1, :],
                                           mxk2[0:1, 1:2], -1.0,
                                           ALU.mult, ALU.mult), ve)

            for j in range(NTB):
                K = 16
                pe_s0 = p.val(pe)
                se_s0 = p.val(se)

                def S_inc(tk, pe_s0=pe_s0):
                    return pe_s0 + (1 if tk == 0 else 2 * tk)

                def Z_inc(tk, K=K, pe_s0=pe_s0):
                    if tk <= K - 3:
                        return pe_s0 + 2 * tk + 3
                    return pe_s0 + 2 * (K - 1) + (tk - (K - 3))

                @block.tensor
                def _(tensor, hp=hp, j=j, K=K, se_s0=se_s0):
                    p.wait(tensor, ve)
                    for tk in range(K):
                        if tk >= 2:
                            tensor.wait_ge(se, se_s0 + 2 * (tk - 1))
                        for hh in range(2):
                            bk = bank(ps_all, 2 * hh + tk % 2)
                            tensor.matmul(bk, ks[hh * 64:(hh + 1) * 64,
                                                 tk * 128:(tk + 1) * 128],
                                          qs[hh * 64:(hh + 1) * 64,
                                             j * 512:(j + 1) * 512],
                                          start=True, stop=False,
                                          tile_position=(64 * hh, 0))
                            i = tensor.matmul(bk, ones_row_bf[0:1, :],
                                              nrow2[0:1, hh, j * 512:(j + 1) * 512],
                                              start=False, stop=True)
                        i.then_inc(pe, 1)
                        if tk >= 1:
                            tkp = tk - 1
                            tensor.wait_ge(se, se_s0 + 2 * tkp + 2)
                            for hh in range(2):
                                tensor.matmul(bank(ps_all, 4)[hh * 64:(hh + 1) * 64, :],
                                              v2[:, tkp, hh * 64:(hh + 1) * 64],
                                              pt2[:, 16 * hh + tkp, :],
                                              start=(tkp == 0), stop=(tkp == K - 1),
                                              tile_position=(0, 64 * hh))
                            tensor.matmul(bank(ps_all, 6)[0:1, :],
                                          lb.ones_col_bf[:, 0:1], pt2[:, tkp, :],
                                          start=(tkp == 0), stop=(tkp == K - 1))
                            tensor.matmul(bank(ps_all, 7)[0:1, :],
                                          lb.ones_col_bf[:, 0:1], pt2[:, 16 + tkp, :],
                                          start=(tkp == 0),
                                          stop=(tkp == K - 1)).then_inc(pe, 1)
                    tkp = K - 1
                    tensor.wait_ge(se, se_s0 + 2 * tkp + 2)
                    for hh in range(2):
                        tensor.matmul(bank(ps_all, 4)[hh * 64:(hh + 1) * 64, :],
                                      v2[:, tkp, hh * 64:(hh + 1) * 64],
                                      pt2[:, 16 * hh + tkp, :],
                                      start=False, stop=True,
                                      tile_position=(0, 64 * hh))
                    tensor.matmul(bank(ps_all, 6)[0:1, :],
                                  lb.ones_col_bf[:, 0:1], pt2[:, tkp, :],
                                  start=False, stop=True)
                    tensor.matmul(bank(ps_all, 7)[0:1, :],
                                  lb.ones_col_bf[:, 0:1], pt2[:, 16 + tkp, :],
                                  start=False, stop=True).then_inc(pe, 1)
                p.bump(pe, 2 * K)

                @block.scalar
                def _(scalar, j=j, K=K, pe_s0=pe_s0):
                    p.wait(scalar, pe, pe_s0)
                    for tk in range(K):
                        scalar.wait_ge(pe, S_inc(tk))
                        scalar.activation(pt2[:, tk, :],
                                          bank(ps_all, tk % 2), AF.Exp).then_inc(se, 1)
                        scalar.activation(pt2[:, 16 + tk, :],
                                          bank(ps_all, 2 + tk % 2),
                                          AF.Exp).then_inc(se, 1)
                p.bump(se, 2 * K)

                @block.vector
                def _(vector, K=K):
                    vector.wait_ge(pe, Z_inc(K - 1))
                    vector.reciprocal(zrow2[0:1, 0, :], bank(ps_all, 6)[0:1, :]).then_inc(ve, 1)
                    vector.reciprocal(zrow2[0:1, 1, :], bank(ps_all, 7)[0:1, :]).then_inc(ve, 1)
                p.bump(ve, 2)

                @block.tensor
                def _(tensor):
                    p.wait(tensor, ve)
                    tensor.matmul(bank(ps_all, 5)[0:64, :], lb.ones_row_f32[0:1, 0:64],
                                  zrow2[0:1, 0, :], start=True, stop=True)
                    p.inc(tensor.matmul(bank(ps_all, 5)[64:128, :],
                                        lb.ones_row_f32[0:1, 0:64],
                                        zrow2[0:1, 1, :], start=True, stop=True,
                                        tile_position=(0, 64)), pe)

                @block.vector
                def _(vector, hp=hp, j=j):
                    p.wait(vector, pe)
                    p.inc(vector.tensor_copy(rsc[:, 2, :], bank(ps_all, 5)), ve)
                    p.wait(vector, ve)
                    p.inc(vector.tensor_tensor(obuf[:, hp, j * 512:(j + 1) * 512],
                                               bank(ps_all, 4), rsc[:, 2, :],
                                               ALU.mult), ve)

        lx_c0 = p.val(lx)
        w_marks = {}

        for j in range(NTB):
            @block.sync
            def _(sync, j=j):
                p.wait(sync, so)
                sync.wait_ge(se, se_cast_all)
                p.inc(sync.dma_start(out=xc[:],
                                     in_=xv[:, :, j * 512:(j + 1) * 512]), lx, 16)

            for mt in range(DT):
                gi = j * DT + mt

                @block.gpsimd
                def _(g, j=j, mt=mt, gi=gi):
                    if gi >= 1:
                        g.wait_ge(pe, w_marks[gi - 1])
                    g.dma_start(out=wor[:],
                                in_=wot_h[:, mt * 128:(mt + 1) * 128]
                                .rearrange("(hp p) m -> p hp m", p=128)).then_inc(lwo, 16)
                p.bump(lwo, 16)

                @block.tensor
                def _(tensor, j=j, mt=mt, gi=gi):
                    if mt == 0:
                        p.wait(tensor, ve)
                        p.wait(tensor, lx, lx_c0 + 16 * (j + 1))
                    tensor.wait_ge(lwo, 16 * (gi + 1))
                    if gi >= 2:
                        p.wait(tensor, ve)
                    bk = bank(ps_all, mt % 2)
                    for hp2 in range(DT):
                        i = tensor.matmul(bk, wor[:, hp2, :],
                                          obuf[:, hp2, j * 512:(j + 1) * 512],
                                          start=(hp2 == 0), stop=(hp2 == DT - 1))
                    i.then_inc(pe, 1)
                p.bump(pe, 1)
                w_marks[gi] = p.val(pe)

                @block.vector
                def _(vector, j=j, mt=mt):
                    p.wait(vector, pe)
                    p.inc(vector.scalar_tensor_tensor(
                        xc[:, mt, :], bank(ps_all, mt % 2), boc[:, mt:mt + 1],
                        xc[:, mt, :], ALU.add, ALU.add), ve)

            emit_ln_chunk(p, block, lb, xc[:], xc[:], sems=sems)

            @block.sync
            def _(sync, j=j):
                p.wait(sync, ve)
                p.inc(sync.dma_start(out=xo[:, :, j * 512:(j + 1) * 512],
                                     in_=xc[:]), so, 16)

        @block.sync
        def _(sync):
            p.wait(sync, so)
    return p


def l3_attn_in_maps(inputs, xs):
    wqt = np.ascontiguousarray(np.asarray(inputs["L3_attn_wq"], np.float32).T)
    wkt = np.ascontiguousarray(np.asarray(inputs["L3_attn_wk"], np.float32).T)
    wvt = np.ascontiguousarray(np.asarray(inputs["L3_attn_wv"], np.float32).T)
    wot = np.ascontiguousarray(np.asarray(inputs["L3_attn_wo"], np.float32).T)
    bqc = _col(inputs["L3_attn_bq"])
    bkc = _col(inputs["L3_attn_bk"])
    bvr = _f32c(inputs["L3_attn_bv"]).reshape(1, D)
    boc = _col(inputs["L3_attn_bo"])
    dk = 64
    freqs = 10000.0 ** (np.arange(0, dk, 2, dtype=np.float64) / dk)
    ang = np.arange(T, dtype=np.float64)[:, None] / freqs[None, :]
    cos64 = np.repeat(np.cos(ang), 2, axis=1).T.astype(np.float32)
    sin64 = np.repeat(np.sin(ang), 2, axis=1).T.astype(np.float32)
    cosb = np.ascontiguousarray(np.concatenate([cos64, cos64], 0))
    sinb = np.ascontiguousarray(np.concatenate([sin64, sin64], 0))
    R = np.zeros((dk, dk), np.float32)
    for i2 in range(0, dk, 2):
        R[i2, i2 + 1] = -1.0
        R[i2 + 1, i2] = 1.0
    R2 = np.zeros((128, 128), np.float32)
    R2[0:64, 0:64] = R
    R2[64:128, 64:128] = R
    r2t = np.ascontiguousarray(R2.T).astype(BF)
    return [{"x_in": xs[c], "wqt": wqt, "wkt": wkt, "wvt": wvt, "wot": wot,
             "bqc": bqc, "bkc": bkc, "bvr": bvr, "boc": boc,
             "cosb": cosb, "sinb": sinb, "r2t": r2t} for c in range(8)]



def build_l0_attn(debug=0):
    p = Prog()
    nc = p.nc
    x_in = nc.dram_tensor("x_in", [D, T], F32, kind="ExternalInput").ap()
    wq_h = nc.dram_tensor("wq", [D, D], F32, kind="ExternalInput").ap()
    wk_h = nc.dram_tensor("wk", [D, D], F32, kind="ExternalInput").ap()
    wv_h = nc.dram_tensor("wv", [D, D], F32, kind="ExternalInput").ap()
    w5_h = nc.dram_tensor("w5", [D, 80], F32, kind="ExternalInput").ap()
    b5_h = nc.dram_tensor("b5", [1, 80], F32, kind="ExternalInput").ap()
    wo_h = nc.dram_tensor("wo", [D, D], F32, kind="ExternalInput").ap()
    eyb_h = nc.dram_tensor("eyeb", [128, 128], BF16, kind="ExternalInput").ap()
    x_out = nc.dram_tensor("x_out", [D, T], F32, kind="ExternalOutput").ap()

    xv = x_in.rearrange("(dt p) t -> p dt t", p=128)
    xo = x_out.rearrange("(dt p) t -> p dt t", p=128)
    NCH = 16

    with p.es:
        xb = p.sb("xb", [128, DT, T], BF16)
        obuf = xb
        qs = p.sb("qs", [128, DT, T], BF16)
        KALL = p.sb("KALL", [128, NCH, D], BF16)
        VALL = p.sb("VALL", [128, NCH, D], BF16)
        xc = p.sb("xc", [128, DT, 512], F32)
        wb1 = p.sb("wb1", [128, DT, D], BF16)
        wb2 = p.sb("wb2", [128, DT, D], BF16)
        w5b = p.sb("w5b", [128, DT, 80], BF16)
        b5r = p.sb("b5r", [1, 80], BF16)
        gt_all = p.sb("gt_all", [128, NCH, 64], F32)
        betac = p.sb("betac", [128, 16], F32)
        kf = xc[:, 0:2].rearrange("p a b -> p (a b)")
        vf = xc[:, 2:4].rearrange("p a b -> p (a b)")
        scr = xc[:, 4:6].rearrange("p a b -> p (a b)")
        scr2 = xc[:, 6:8].rearrange("p a b -> p (a b)")
        skk = p.sb("skk", [128, NCH, 16], F32)
        svv = p.sb("svv", [128, NCH, 16], F32)
        skv = p.sb("skv", [128, NCH, 16], F32)
        psi = None
        surpb = p.sb("surpb", [128, NCH, 16], BF16)
        fsall = p.sb("fsall", [128, NCH, 32], BF16)
        cfa = skk
        csa = svv
        sarr = skv
        lnrows = p.sb("lnrows", [1, 2048], F32)
        srow = lnrows[0:1, 0:512]
        ksb = p.sb("ksb", [128, D], BF16)
        psi = ksb[:].bitcast(F32)[:, 0:256].rearrange("p (a b) -> p a b", b=16)
        vsb = p.sb("vsb", [128, D], BF16)
        stf = p.sb("stf", [128, 1024], BF16)
        sts = p.sb("sts", [128, 1024], BF16)
        eyeb = p.sb("eyeb_s", [128, 128], BF16)
        cc3 = p.sb("cc3", [128, 3], F32)
        c3 = cc3[:, 0:1]
        c10 = cc3[:, 1:2]
        cm5 = cc3[:, 2:3]
        ones_row_bf = p.sb("ones_row_bf", [1, 128], BF16)
        ob = p.sb("ob", [128, D], F32)
        ps_all = p.psum("ps_all", [128, 4096], F32)
        wor = KALL[:, 0, :].rearrange("p (et m) -> p et m", m=128)
        lb = LnBufs(p, ps_all, tb=qs[:, :, 0:512], sq=qs[:, :, 512:1024],
                    rows=lnrows[0:1, :],
                    scr=ob[:].rearrange("p (a b) -> p a b", a=2))

        ld = p.sem("ld")
        lg = p.sem("lg")
        lw = p.sem("lw")
        lwo = p.sem("lwo")
        lx = p.sem("lx")
        pe = p.sem("pe")
        se = p.sem("se")
        ve = p.sem("ve")
        so = p.sem("so")
        sems = {"pe": pe, "se": se, "ve": ve}

        block = p.es.enter_context(nc.Block())

        @block.gpsimd
        def _(g):
            p.inc(g.dma_start(out=b5r[:], in_=b5_h[:, :]), ld, 16)
            for dt_ in range(DT):
                p.inc(g.dma_start(out=wb1[:, dt_, :],
                                  in_=wq_h[dt_ * 128:(dt_ + 1) * 128, :]), lg, 16)
            p.inc(g.dma_start(out=eyeb[:], in_=eyb_h[:, :]), lg, 16)
            for dt_ in range(DT):
                p.inc(g.dma_start(out=w5b[:, dt_, :],
                                  in_=w5_h[dt_ * 128:(dt_ + 1) * 128, :]), lg, 16)
            for dt_ in range(DT):
                p.inc(g.dma_start(out=wb2[:, dt_, :],
                                  in_=wv_h[dt_ * 128:(dt_ + 1) * 128, :]), lg, 16)

        emit_consts(p, block, lb, ve)

        @block.vector
        def _(vector):
            vector.memset(stf[:], 0.0)
            vector.memset(sts[:], 0.0)
            vector.memset(c3, 3.0)
            vector.memset(c10, 10.0)
            vector.memset(cm5, -5.0)
            p.inc(vector.memset(ones_row_bf[:], 1.0), ve)

        for tb in range(NTB):
            @block.sync
            def _(sync, tb=tb):
                p.wait(sync, se)
                p.inc(sync.dma_start(out=xc[:],
                                     in_=xv[:, :, tb * 512:(tb + 1) * 512]), lx, 16)

            @block.scalar
            def _(scalar, tb=tb):
                p.wait(scalar, lx, 16 * (tb + 1))
                p.inc(scalar.activation(xb[:, :, tb * 512:(tb + 1) * 512],
                                        xc[:], AF.Copy), se)

        se_cast_all = p.val(se)

        pe_q0 = p.val(pe)
        se_q0 = p.val(se)

        @block.tensor
        def _(tensor):
            p.wait(tensor, lg)
            p.wait(tensor, ld, None)
            tensor.wait_ge(se, se_cast_all)
            for w in range(32):
                ft, tj = divmod(w, 4)
                if w >= 2:
                    tensor.wait_ge(se, se_q0 + (w - 1))
                bk = bank(ps_all, w % 2)
                for dt_ in range(DT):
                    i = tensor.matmul(bk, wb1[:, dt_, ft * 128:(ft + 1) * 128],
                                      xb[:, dt_, tj * 512:(tj + 1) * 512],
                                      start=(dt_ == 0), stop=(dt_ == DT - 1))
                i.then_inc(pe, 1)
        p.bump(pe, 32)

        @block.scalar
        def _(scalar, pe_q0=pe_q0):
            for w in range(32):
                ft, tj = divmod(w, 4)
                scalar.wait_ge(pe, pe_q0 + w + 1)
                scalar.activation(qs[:, ft, tj * 512:(tj + 1) * 512],
                                  bank(ps_all, w % 2), AF.Silu).then_inc(se, 1)
        p.bump(se, 32)

        pe_qdone = p.val(pe)

        @block.gpsimd
        def _(g):
            g.wait_ge(pe, pe_qdone)
            for dt_ in range(DT):
                p.inc(g.dma_start(out=wb1[:, dt_, :],
                                  in_=wk_h[dt_ * 128:(dt_ + 1) * 128, :]), lw, 16)

        for c in range(NCH):
            pe_c0 = p.val(pe)
            se_c0 = p.val(se)
            ve_c0 = p.val(ve)

            @block.tensor
            def _(tensor, c=c):
                if c == 0:
                    p.wait(tensor, lw)
                    p.wait(tensor, se)
                else:
                    p.wait(tensor, ve)
                cs = (c * 128, (c + 1) * 128)
                for g5 in range(5):
                    bk = bank(ps_all, g5) if g5 < 4 else bank(ps_all, 4, 80)
                    for dt_ in range(DT):
                        if g5 < 2:
                            rhs = wb1[:, dt_, g5 * 512:(g5 + 1) * 512]
                        elif g5 < 4:
                            rhs = wb2[:, dt_, (g5 - 2) * 512:(g5 - 1) * 512]
                        else:
                            rhs = w5b[:, dt_, :]
                        i = tensor.matmul(bk, xb[:, dt_, cs[0]:cs[1]], rhs,
                                          start=(dt_ == 0), stop=(dt_ == DT - 1 and g5 != 4))
                    if g5 == 4:
                        i = tensor.matmul(bk, ones_row_bf[0:1, cs[0] - c * 128:cs[1] - c * 128]
                                          if False else ones_row_bf[0:1, :],
                                          b5r[0:1, :], start=False, stop=True)
                    i.then_inc(pe, 1)
            p.bump(pe, 5)

            @block.scalar
            def _(scalar, c=c, pe_c0=pe_c0, ve_c0=ve_c0):
                for g5 in range(4):
                    scalar.wait_ge(pe, pe_c0 + g5 + 1)
                    if g5 >= 1:
                        scalar.wait_ge(ve, ve_c0 + g5)
                    scalar.activation(scr[:, (g5 % 2) * 512:(g5 % 2) * 512 + 512],
                                      bank(ps_all, g5), AF.Sigmoid).then_inc(se, 1)
                scalar.wait_ge(pe, pe_c0 + 5)
                if c >= 1:
                    scalar.wait_ge(ve, ve_c0 - 8)
                scalar.activation(betac[:], bank(ps_all, 4, 80)[:, 0:16],
                                  AF.Sigmoid)
                scalar.activation(gt_all[:, c, :], bank(ps_all, 4, 80)[:, 16:80],
                                  AF.Sigmoid).then_inc(se, 1)
            p.bump(se, 5)

            @block.vector
            def _(vector, c=c, se_c0=se_c0, ve_c0=ve_c0):
                for g5 in range(4):
                    vector.wait_ge(se, se_c0 + g5 + 1)
                    dst = kf if g5 < 2 else vf
                    vector.tensor_tensor(dst[:, (g5 % 2) * 512:(g5 % 2) * 512 + 512],
                                         scr[:, (g5 % 2) * 512:(g5 % 2) * 512 + 512],
                                         bank(ps_all, g5), ALU.mult).then_inc(ve, 1)
                vector.wait_ge(se, se_c0 + 5)
                bb = betac[:].unsqueeze(2).broadcast_to([128, 16, 64])
                kv = kf[:].rearrange("p (h e) -> p h e", e=64)
                vv = vf[:].rearrange("p (h e) -> p h e", e=64)
                vector.wait_ge(ve, ve_c0 + 2)
                vector.tensor_tensor(kv, kv, bb, ALU.mult).then_inc(ve, 1)
                vector.wait_ge(ve, ve_c0 + 4)
                vector.tensor_tensor(vv, vv, bb, ALU.mult).then_inc(ve, 1)
                p.wait(vector, ve)
                s2v = scr2[:].rearrange("p (h e) -> p h e", e=64)
                vector.tensor_mul(scr2[:], kf[:], kf[:]).then_inc(ve, 1)
                p.wait(vector, ve)
                vector.tensor_reduce(skk[:, c, :].unsqueeze(2), s2v, AX.X, ALU.add).then_inc(ve, 1)
                vector.tensor_mul(scr2[:], vf[:], vf[:]).then_inc(ve, 1)
                p.wait(vector, ve)
                vector.tensor_reduce(svv[:, c, :].unsqueeze(2), s2v, AX.X, ALU.add).then_inc(ve, 1)
                vector.tensor_mul(scr2[:], kf[:], vf[:]).then_inc(ve, 1)
                p.wait(vector, ve)
                vector.tensor_reduce(skv[:, c, :].unsqueeze(2), s2v, AX.X, ALU.add).then_inc(ve, 1)
                vector.tensor_copy(KALL[:, c, :], kf[:]).then_inc(ve, 1)
                vector.tensor_copy(VALL[:, c, :], vf[:]).then_inc(ve, 1)
            p.bump(ve, 14)

        @block.vector
        def _(vector):
            p.wait(vector, ve)
            vector.tensor_mul(psi[:], skk[:], svv[:]).then_inc(ve, 1)
            p.wait(vector, ve)
            vector.tensor_single_scalar(psi[:], psi[:], 1e-30, ALU.add).then_inc(ve, 1)
        p.bump(ve, 2)

        @block.scalar
        def _(scalar):
            p.wait(scalar, ve)
            p.inc(scalar.activation(psi[:], psi[:], AF.Ln), se)
            p.wait(scalar, se)
            p.inc(scalar.activation(psi[:], psi[:], AF.Exp,
                                    scale=lb.cnm05[:, 0:1]), se)
            p.inc(scalar.activation(scr[:, 0:256],
                                    skv[:].rearrange("p a b -> p (a b)"),
                                    AF.Abs), se)

        @block.vector
        def _(vector):
            p.wait(vector, se)
            vector.tensor_mul(psi[:], psi[:],
                              scr[:, 0:256].rearrange("p (a b) -> p a b", b=16)).then_inc(ve, 1)
        p.bump(ve, 1)

        @block.scalar
        def _(scalar):
            p.wait(scalar, ve)
            p.inc(scalar.activation(psi[:], psi[:], AF.Sigmoid, scale=c3), se)
            p.wait(scalar, se)
            p.inc(scalar.activation(surpb[:], psi[:], AF.Sigmoid,
                                    scale=c10, bias=cm5), se)

        @block.vector
        def _(vector):
            p.wait(vector, se)
            vector.tensor_scalar(scr[:, 0:256], psi[:].rearrange("p a b -> p (a b)"),
                                 0.3, 0.5, ALU.mult, ALU.add).then_inc(ve, 1)
            p.wait(vector, ve)
            vector.tensor_mul(cfa[:], scr[:, 0:256].rearrange("p (a b) -> p a b", b=16),
                              gt_all[:, :, 32:48]).then_inc(ve, 1)
            vector.tensor_scalar(scr[:, 256:512], scr[:, 0:256],
                                 -1.0, 1.0, ALU.mult, ALU.add).then_inc(ve, 1)
            p.wait(vector, ve)
            vector.tensor_mul(csa[:], scr[:, 256:512].rearrange("p (a b) -> p a b", b=16),
                              gt_all[:, :, 48:64]).then_inc(ve, 1)
            vector.tensor_scalar(scr[:, 0:256], psi[:].rearrange("p a b -> p (a b)"),
                                 -0.1, 1.0, ALU.mult, ALU.add).then_inc(ve, 1)
            p.wait(vector, ve)
            vector.tensor_mul(fsall[:, :, 0:16],
                              scr[:, 0:256].rearrange("p (a b) -> p a b", b=16),
                              gt_all[:, :, 0:16]).then_inc(ve, 1)
            vector.tensor_scalar(scr[:, 256:512], psi[:].rearrange("p a b -> p (a b)"),
                                 -0.05, 1.0, ALU.mult, ALU.add).then_inc(ve, 1)
            p.wait(vector, ve)
            vector.tensor_mul(fsall[:, :, 16:32],
                              scr[:, 256:512].rearrange("p (a b) -> p a b", b=16),
                              gt_all[:, :, 16:32]).then_inc(ve, 1)
        p.bump(ve, 8)

        pe_d0 = p.val(pe)

        @block.tensor
        def _(tensor):
            p.wait(tensor, ve)
            for c in range(NCH):
                i = tensor.matmul(ps_all[0:1, 7 * 512 + c * 32: 7 * 512 + (c + 1) * 32],
                                  lb.ones_col_bf[:, 0:1], fsall[:, c, :],
                                  start=True, stop=True)
            p.inc(i, pe)

        @block.vector
        def _(vector):
            p.wait(vector, pe)
            p.inc(vector.tensor_single_scalar(srow[0:1, :], ps_all[0:1, 7 * 512:8 * 512],
                                              1.0 / 128.0, ALU.mult), ve)

        @block.scalar
        def _(scalar):
            p.wait(scalar, ve)
            for it in range(7):
                p.inc(scalar.activation(srow[0:1, :], srow[0:1, :], AF.Square), se)
                p.wait(scalar, se)

        @block.tensor
        def _(tensor):
            p.wait(tensor, se)
            p.inc(tensor.matmul(bank(ps_all, 7), lb.ones_row_f32[0:1, :],
                                srow[0:1, :], start=True, stop=True), pe)

        @block.vector
        def _(vector):
            p.wait(vector, pe)
            sb7 = bank(ps_all, 7)
            for half in range(2):
                pp = slice(half * 64, (half + 1) * 64)
                src_f = sb7[pp, :].rearrange("p (c k) -> p c k", k=32)[:, :, half:16:2]
                src_s = sb7[pp, :].rearrange("p (c k) -> p c k", k=32)[:, :, 16 + half:32:2]
                vector.tensor_copy(sarr[pp, :, 0:8], src_f).then_inc(ve, 1)
                vector.tensor_copy(sarr[pp, :, 8:16], src_s).then_inc(ve, 1)
        p.bump(ve, 4)


        if debug == 1:
            @block.gpsimd
            def _(g):
                p.wait(g, ve)
                p.wait(g, se)
                p.wait(g, pe)
                p.inc(g.dma_start(out=xo[:, 0:4, 0:512], in_=xb[:, 0:4, 0:512]), so, 16)

            @block.sync
            def _(sync):
                p.wait(sync, so)
            return p
        zrow_bf = lnrows[0:1, 512:768].bitcast(BF16)

        @block.vector
        def _(vector):
            p.inc(vector.memset(zrow_bf, 0.0), ve)

        @block.tensor
        def _(tensor):
            p.wait(tensor, ve)
            tensor.matmul(bank(ps_all, 5), ones_row_bf[0:1, :], zrow_bf,
                          start=True, stop=False, skip_group_check=True)
            p.inc(tensor.matmul(bank(ps_all, 6), ones_row_bf[0:1, :],
                                zrow_bf, start=True, stop=False,
                                skip_group_check=True), pe)

        SCANP = 9
        SCANC = NCH
        for c in range(min(SCANC, NCH)):
            ve_s0 = p.val(ve)
            pe_s0 = p.val(pe)
            se_s0 = p.val(se)

            @block.vector
            def _(vector, c=c):
                if c == 0:
                    p.wait(vector, ve)
                else:
                    p.wait(vector, pe)
                if c > 0:
                    sfv = bank(ps_all, 5).rearrange("p (hp e) -> p hp e", e=64)
                    ssv = bank(ps_all, 6).rearrange("p (hp e) -> p hp e", e=64)
                    bf = sarr[:, c, 0:8].unsqueeze(2).broadcast_to([128, 8, 64])
                    bs = sarr[:, c, 8:16].unsqueeze(2).broadcast_to([128, 8, 64])
                    vector.tensor_tensor(sfv, sfv, bf, ALU.mult).then_inc(ve, 1)
                    vector.tensor_tensor(ssv, ssv, bs, ALU.mult).then_inc(ve, 1)
                else:
                    vector.nop().then_inc(ve, 1)
                    vector.nop().then_inc(ve, 1)
                sbv = surpb[:, c, :].unsqueeze(2).broadcast_to([128, 16, 64])
                vector.tensor_tensor(ksb[:].rearrange("p (h e) -> p h e", e=64),
                                     KALL[:, c, :].rearrange("p (h e) -> p h e", e=64),
                                     sbv, ALU.mult).then_inc(ve, 1)
                vector.tensor_tensor(vsb[:].rearrange("p (h e) -> p h e", e=64),
                                     VALL[:, c, :].rearrange("p (h e) -> p h e", e=64),
                                     sbv, ALU.mult).then_inc(ve, 1)
            p.bump(ve, 4)

            if SCANP < 2:
                continue
            @block.tensor
            def _(tensor, c=c, ve_s0=ve_s0):
                tensor.wait_ge(ve, ve_s0 + 2)
                for h in range(16):
                    hp, lo = divmod(h, 2)
                    tensor.matmul(bank(ps_all, 5)[lo * 64:(lo + 1) * 64,
                                                  hp * 64:(hp + 1) * 64],
                                  VALL[:, c, h * 64:(h + 1) * 64],
                                  KALL[:, c, h * 64:(h + 1) * 64],
                                  start=False, stop=False,
                                  skip_group_check=True,
                                  tile_position=(0, 64 * lo))
                tensor.wait_ge(ve, ve_s0 + 4)
                for h in range(16):
                    hp, lo = divmod(h, 2)
                    i = tensor.matmul(bank(ps_all, 6)[lo * 64:(lo + 1) * 64,
                                                      hp * 64:(hp + 1) * 64],
                                      vsb[:, h * 64:(h + 1) * 64],
                                      ksb[:, h * 64:(h + 1) * 64],
                                      start=False, stop=False,
                                      skip_group_check=True,
                                      tile_position=(0, 64 * lo))
                p.inc(i, pe)

            @block.scalar
            def _(scalar, c=c):
                p.wait(scalar, pe)
                for st2, bno in ((stf, 5), (sts, 6)):
                    scalar.activation(
                        st2[0:64, :].rearrange("p (a b) -> p a b", b=64)[:, 0:16:2, :],
                        bank(ps_all, bno)[0:64, :].rearrange("p (a b) -> p a b", b=64),
                        AF.Copy)
                    scalar.activation(
                        st2[64:128, :].rearrange("p (a b) -> p a b", b=64)[:, 1:16:2, :],
                        bank(ps_all, bno)[64:128, :].rearrange("p (a b) -> p a b", b=64),
                        AF.Copy).then_inc(se, 1)
            p.bump(se, 2)

            if SCANP < 3:
                continue
            @block.tensor
            def _(tensor, c=c, se_s0=se_s0, ve_s0=ve_s0):
                tensor.wait_ge(se, se_s0 + 2)
                p.wait(tensor, ve, ve_s0)
                for rnd in range(4):
                    fast = rnd < 2
                    hbase = 0 if rnd % 2 == 0 else 8
                    bk = bank(ps_all, 7 if rnd % 2 == 0 else 4)
                    if rnd >= 2:
                        tensor.wait_ge(ve, ve_s0 + 4 + rnd - 1)
                    st = stf if fast else sts
                    for hh in range(8):
                        h = hbase + hh
                        hp, lo = divmod(h, 2)
                        i = tensor.matmul(bk[:, hh * 64:(hh + 1) * 64],
                                          qs[:, hp, c * 128:(c + 1) * 128],
                                          st[:, h * 64:(h + 1) * 64],
                                          start=True, stop=True)
                    i.then_inc(pe, 1)
            p.bump(pe, 4)

            oc = VALL[:, c, :]

            if SCANP < 4:
                continue
            @block.vector
            def _(vector, c=c, oc=oc, pe_s0=pe_s0, ve_s0=ve_s0):
                for rnd in range(4):
                    fast = rnd < 2
                    hbase = 0 if rnd % 2 == 0 else 8
                    bk = bank(ps_all, 7 if rnd % 2 == 0 else 4)
                    vector.wait_ge(pe, pe_s0 + 2 + rnd)
                    co = (cfa if fast else csa)[:, c, hbase // 1:0] if False else None
                    cview = (cfa if fast else csa)[:, c, (hbase // 8) * 8: (hbase // 8) * 8 + 8] \
                        if False else (cfa if fast else csa)[:, c, hbase:hbase + 8]
                    cb = cview.unsqueeze(2).broadcast_to([128, 8, 8 * 8])
                    cb = cview.unsqueeze(2).broadcast_to([128, 8, 64])
                    bkv = bk.rearrange("p (h e) -> p h e", e=64)
                    obv = ob[:, hbase * 64:(hbase + 8) * 64].rearrange(
                        "p (h e) -> p h e", e=64)
                    if fast:
                        vector.tensor_tensor(obv, bkv, cb, ALU.mult).then_inc(ve, 1)
                    else:
                        vector.tensor_tensor(scr[:, 0:512].rearrange(
                            "p (h e) -> p h e", e=64), bkv, cb, ALU.mult).then_inc(ve, 1)
                        vector.wait_ge(ve, ve_s0 + 3 + 2 * rnd)
                        vector.tensor_add(oc[:, hbase * 64:(hbase + 8) * 64],
                                          ob[:, hbase * 64:(hbase + 8) * 64],
                                          scr[:, 0:512]).then_inc(ve, 1)
            p.bump(ve, 6)

            if SCANP < 5:
                continue
            pe_t0 = p.val(pe)

            @block.tensor
            def _(tensor, c=c, oc=oc, ve_s0=ve_s0):
                p.wait(tensor, ve)
                p.wait(tensor, se)
                bkb = bank(ps_all, 0).bitcast(BF16)
                for et in range(DT):
                    i = tensor.transpose(bkb[:, et * 128:(et + 1) * 128],
                                         oc[et * 128 if False else slice(None),
                                            et * 128:(et + 1) * 128] if False else
                                         oc[:, et * 128:(et + 1) * 128],
                                         eyeb[:, :])
                p.inc(i, pe)

            @block.scalar
            def _(scalar, c=c):
                p.wait(scalar, pe)
                bkb = bank(ps_all, 0).bitcast(BF16)
                scalar.activation(obuf[:].rearrange("p a t -> p a t")[:, :, c * 128:(c + 1) * 128],
                                  bkb.rearrange("p (a q) -> p a q", q=128),
                                  AF.Copy).then_inc(se, 1)
            p.bump(se, 1)


        if debug == 2:
            @block.gpsimd
            def _(g):
                p.wait(g, ve)
                p.wait(g, se)
                p.wait(g, pe)
                p.inc(g.dma_start(out=xo[:, 0:4, 0:512], in_=xb[:, 0:4, 0:512]), so, 16)

            @block.sync
            def _(sync):
                p.wait(sync, so)
            return p
        lx_c0 = p.val(lx)
        w_marks = {}

        for j in range(NTB):
            @block.sync
            def _(sync, j=j):
                p.wait(sync, so)
                p.wait(sync, ve)
                p.wait(sync, se)
                p.inc(sync.dma_start(out=xc[:],
                                     in_=xv[:, :, j * 512:(j + 1) * 512]), lx, 16)

            for mt in range(DT):
                gi = j * DT + mt

                @block.gpsimd
                def _(g, j=j, mt=mt, gi=gi):
                    if gi == 0:
                        p.wait(g, pe)
                    else:
                        g.wait_ge(pe, w_marks[gi - 1])
                    g.dma_start(out=wor[:],
                                in_=wo_h[:, mt * 128:(mt + 1) * 128]
                                .rearrange("(et p) m -> p et m", p=128)).then_inc(lwo, 16)
                p.bump(lwo, 16)

                @block.tensor
                def _(tensor, j=j, mt=mt, gi=gi):
                    if mt == 0:
                        p.wait(tensor, se)
                        p.wait(tensor, ve)
                        p.wait(tensor, lx, lx_c0 + 16 * (j + 1))
                    tensor.wait_ge(lwo, 16 * (gi + 1))
                    if gi >= 2:
                        p.wait(tensor, ve)
                    bk = bank(ps_all, 2 + mt % 2)
                    for et in range(DT):
                        i = tensor.matmul(bk, wor[:, et, :],
                                          obuf[:, et, j * 512:(j + 1) * 512],
                                          start=(et == 0), stop=(et == DT - 1))
                    i.then_inc(pe, 1)
                p.bump(pe, 1)
                w_marks[gi] = p.val(pe)

                @block.vector
                def _(vector, j=j, mt=mt):
                    p.wait(vector, pe)
                    p.inc(vector.scalar_tensor_tensor(
                        xc[:, mt, :], bank(ps_all, 2 + mt % 2), 0.0,
                        xc[:, mt, :], ALU.add, ALU.add), ve)

            emit_ln_chunk(p, block, lb, xc[:], xc[:], sems=sems,
                          stat_banks=(0, 1), bc_banks=(4, 5))

            @block.sync
            def _(sync, j=j):
                p.wait(sync, ve)
                p.inc(sync.dma_start(out=xo[:, :, j * 512:(j + 1) * 512],
                                     in_=xc[:]), so, 16)

        @block.sync
        def _(sync):
            p.wait(sync, so)
    return p


def l0_attn_in_maps(inputs, xs):
    wq = _f32c(inputs["L0_attn_Wq"])
    wk = _f32c(inputs["L0_attn_Wk"])
    wv = _f32c(inputs["L0_attn_Wv"])
    wo = _f32c(inputs["L0_attn_Wo"])
    w5 = np.ascontiguousarray(np.concatenate(
        [np.asarray(inputs["L0_attn_Wb"], np.float32),
         np.asarray(inputs["L0_attn_Wfd"], np.float32),
         np.asarray(inputs["L0_attn_Wsd"], np.float32),
         np.asarray(inputs["L0_attn_Wfg"], np.float32),
         np.asarray(inputs["L0_attn_Wsg"], np.float32)], axis=1))
    b5 = np.zeros((1, 80), np.float32)
    b5[0, 16:32] = np.asarray(inputs["L0_attn_fd_bias"], np.float32)
    b5[0, 32:48] = np.asarray(inputs["L0_attn_sd_bias"], np.float32)
    eyb = np.eye(128, dtype=BF)
    return [{"x_in": xs[c], "wq": wq, "wk": wk, "wv": wv, "wo": wo,
             "w5": w5, "b5": b5, "eyeb": eyb} for c in range(8)]




def build_l1_attn(debug=0):
    p = Prog()
    nc = p.nc
    x_in = nc.dram_tensor("x_in", [D, T], F32, kind="ExternalInput").ap()
    wwt_h = nc.dram_tensor("wwt", [D, D], F32, kind="ExternalInput").ap()
    wrt_h = nc.dram_tensor("wrt", [D, D], F32, kind="ExternalInput").ap()
    gin_h = nc.dram_tensor("gin", [1, 8], F32, kind="ExternalInput").ap()
    difU_h = nc.dram_tensor("difU", [64, 64], F32, kind="ExternalInput").ap()
    mskU_h = nc.dram_tensor("mskU", [64, 64], F32, kind="ExternalInput").ap()
    dift_h = nc.dram_tensor("dift", [16, 80], F32, kind="ExternalInput").ap()
    m16_h = nc.dram_tensor("m16", [16, 32], F32, kind="ExternalInput").ap()
    sel_h = nc.dram_tensor("selb", [16, 256], BF16, kind="ExternalInput").ap()
    ioc_h = nc.dram_tensor("ioc", [128, 1], F32, kind="ExternalInput").ap()
    ior_h = nc.dram_tensor("ior", [1, 64], F32, kind="ExternalInput").ap()
    eyb_h = nc.dram_tensor("eyeb", [128, 128], BF16, kind="ExternalInput").ap()
    eyf_h = nc.dram_tensor("eyf", [16, 16], F32, kind="ExternalInput").ap()
    x_out = nc.dram_tensor("x_out", [D, T], F32, kind="ExternalOutput").ap()

    xv = x_in.rearrange("(dt p) t -> p dt t", p=128)
    xo = x_out.rearrange("(dt p) t -> p dt t", p=128)
    NCH = 32

    with p.es:
        rkT = p.sb("rkT", [128, DT, T + 8], BF16)
        vtm = p.sb("vtm", [128, 16, D], BF16)
        wtm = p.sb("wtm", [128, 16, D], BF16)
        wwb = p.sb("wwb", [128, DT, D], BF16)
        ILT = p.sb("ILT", [64, NCH, 256], BF16)
        PW = p.sb("PW", [128, 4096], BF16)
        ZW = p.sb("ZW", [128, 4096], BF16)
        xc = p.sb("xc", [128, DT, 512], F32)
        xbc = p.sb("xbc", [128, DT, 512], BF16)
        vT_s = p.sb("vT_s", [128, DT, 512], BF16)
        Sb = p.sb("Sb", [128, 8, 256], BF16)
        ub = p.sb("ub", [128, 1024], BF16)
        vnewb = p.sb("vnewb", [64, 4, 256], BF16)
        wkg = p.sb("wkg", [64, 4, 256], BF16)
        rks = p.sb("rks", [128, 4, 2, 64], BF16)
        gpb = p.sb("gpb", [128, 256], BF16)
        CUP = p.sb("CUP", [64, 4, 64], F32)
        CLd = p.sb("CLd", [16, 4, 16], F32)
        CUd = p.sb("CUd", [16, 4, 16], F32)
        OD = p.sb("OD", [16, 3, 4, 16], F32)
        SELb = p.sb("SELb", [16, 256], BF16)
        difU = p.sb("difU_s", [64, 64], F32)
        mskU = p.sb("mskU_s", [64, 64], F32)
        dift = p.sb("dift_s", [16, 80], F32)
        m16 = p.sb("m16_s", [16, 32], F32)
        iotc = p.sb("iotc", [128, 1], F32)
        iotr = p.sb("iotr", [1, 64], F32)
        eyeb = p.sb("eyeb_s", [128, 128], BF16)
        eyf = p.sb("eyf_s", [16, 16], F32)
        gin = p.sb("gin_s", [1, 8], F32)
        grow = p.sb("grow", [1, 16], F32)
        gbc = p.sb("gbc", [128, 12], F32)
        gbcb = p.sb("gbcb", [128, 4], BF16)
        gcol = p.sb("gcol", [128, 8], F32)
        gpr = p.sb("gpr", [1, 256], F32)
        cms = p.sb("cms", [1, 4], F32)
        cm1 = p.sb("cm1", [128, 1], F32)
        onef = p.sb("onef", [128, 1], F32)
        ones_row_bf = p.sb("ones_row_bf", [1, 128], BF16)
        zrow_bf = p.sb("zrow_bf", [1, 512], BF16)
        ps_all = p.psum("ps_all", [128, 4096], F32)
        lb = LnBufs(p, ps_all,
                    tb=PW[:, :].rearrange("p (a b) -> p a b", b=512),
                    sq=ZW[:, :].rearrange("p (a b) -> p a b", b=512))
        ATv = xc[:, :, :].bitcast(BF16).rearrange("p a b -> p (a b)") \
            .rearrange("p (c k) -> p c k", k=256)

        ld = p.sem("ld")
        lg = p.sem("lg")
        pe = p.sem("pe")
        se = p.sem("se")
        ve = p.sem("ve")
        so = p.sem("so")
        sems = {"pe": pe, "se": se, "ve": ve}

        block = p.es.enter_context(nc.Block())

        psP = ps_all[0:64, 0:2048]
        psQ = ps_all[0:64, 2048:4096]

        @block.sync
        def _(sync):
            p.inc(sync.dma_start(out=difU[:, :], in_=difU_h), ld, 16)
            p.inc(sync.dma_start(out=mskU[:, :], in_=mskU_h), ld, 16)
            p.inc(sync.dma_start(out=dift[:, :], in_=dift_h), ld, 16)
            p.inc(sync.dma_start(out=m16[:, :], in_=m16_h), ld, 16)
            p.inc(sync.dma_start(out=SELb[:, :], in_=sel_h), ld, 16)
            p.inc(sync.dma_start(out=iotc[:, :], in_=ioc_h), ld, 16)
            p.inc(sync.dma_start(out=iotr[:, :], in_=ior_h), ld, 16)
            p.inc(sync.dma_start(out=eyeb[:, :], in_=eyb_h), ld, 16)
            p.inc(sync.dma_start(out=gin[:, :], in_=gin_h), ld, 16)
            p.inc(sync.dma_start(out=eyf[:, :], in_=eyf_h), ld, 16)

        @block.gpsimd
        def _(g):
            for dt_ in range(DT):
                p.inc(g.dma_start(out=wwb[:, dt_, :],
                                  in_=wwt_h[dt_ * 128:(dt_ + 1) * 128, :]),
                      lg, 16)

        emit_consts(p, block, lb, ve)

        @block.vector
        def _(vector):
            vector.memset(zrow_bf[:], 0.0)
            vector.memset(rkT[:, :, 0:1], 0.0)
            vector.memset(cm1[:], -1.0)
            vector.memset(onef[:], 1.0)
            vector.memset(ones_row_bf[:], 1.0)
            vector.memset(cms[0:1, 0:1], 0.5)
            p.inc(vector.memset(cms[0:1, 1:2], 64.0), ve)

        @block.scalar
        def _(scalar):
            p.wait(scalar, ld)
            p.wait(scalar, ve)
            scalar.activation(grow[0:1, 0:4], gin[0:1, 0:4], AF.Sigmoid)
            p.inc(scalar.activation(grow[0:1, 8:12], gin[0:1, 4:8], AF.Exp), se)
            p.wait(scalar, se)
            p.inc(scalar.activation(grow[0:1, 4:8], grow[0:1, 0:4], AF.Ln), se)
            p.wait(scalar, se)
            p.inc(scalar.activation(grow[0:1, 12:16], grow[0:1, 4:8], AF.Exp,
                                    scale=cms[0:1, 1:2]), se)

        @block.tensor
        def _(tensor):
            p.wait(tensor, se)
            p.inc(tensor.matmul(ps_all[0:128, 0:12], lb.ones_row_f32[0:1, :],
                                grow[0:1, 4:16], start=True, stop=True), pe)

        @block.vector
        def _(vector):
            p.wait(vector, pe)
            p.inc(vector.tensor_copy(gbc[:, :], ps_all[0:128, 0:12]), ve)

        @block.scalar
        def _(scalar):
            p.wait(scalar, ve)
            p.inc(scalar.activation(gbcb[:, :], gbc[:, 4:8], AF.Copy), se)

        for h in range(4):
            @block.scalar
            def _(scalar, h=h):
                if h >= 2:
                    p.wait(scalar, ve)
                scalar.activation(lb.scr[0:64, h % 2, 0:64], difU[:, :],
                                  AF.Exp, scale=gbc[0:64, h:h + 1])
                scalar.activation(CLd[:, h, :], dift[:, 0:16], AF.Exp,
                                  scale=gbc[0:16, h:h + 1])
                scalar.activation(CUd[:, h, :], dift[:, 16:32], AF.Exp,
                                  scale=gbc[0:16, h:h + 1])
                for dd in range(3):
                    scalar.activation(OD[:, dd, h, :],
                                      dift[:, 32 + 16 * dd:48 + 16 * dd],
                                      AF.Exp, scale=gbc[0:16, h:h + 1])
                scalar.activation(gcol[:, 4 + h:5 + h], iotc[:, 0:1], AF.Exp,
                                  scale=gbc[:, h:h + 1])
                p.inc(scalar.activation(gpr[0:1, h * 64:(h + 1) * 64],
                                        iotr[0:1, :], AF.Exp,
                                        scale=gbc[0:1, h:h + 1]), se)

            @block.vector
            def _(vector, h=h):
                p.wait(vector, se)
                vector.tensor_tensor(CUP[:, h, :], lb.scr[0:64, h % 2, 0:64],
                                     mskU[:, :], ALU.mult)
                vector.tensor_tensor(CLd[:, h, :], CLd[:, h, :],
                                     m16[:, 0:16], ALU.mult)
                p.inc(vector.tensor_tensor(CUd[:, h, :], CUd[:, h, :],
                                           m16[:, 16:32], ALU.mult), ve)

        @block.vector
        def _(vector):
            p.wait(vector, se)
            p.inc(vector.tensor_single_scalar(
                OD[:, :, :, :].rearrange("p a b c -> p (a b c)"),
                OD[:, :, :, :].rearrange("p a b c -> p (a b c)"),
                -1.0, ALU.mult), ve)

        @block.tensor
        def _(tensor):
            p.wait(tensor, se)
            p.wait(tensor, ve)
            for h in range(4):
                i = tensor.matmul(ps_all[0:128, 1024 + h * 64:1024 + (h + 1) * 64],
                                  lb.ones_row_f32[0:1, :],
                                  gpr[0:1, h * 64:(h + 1) * 64],
                                  start=True, stop=True)
            p.inc(i, pe)

        @block.scalar
        def _(scalar):
            p.wait(scalar, pe)
            p.inc(scalar.activation(gpb[:, :], ps_all[0:128, 1024:1280],
                                    AF.Copy), se)

        for tj in range(NTB):
            t0 = tj * 512

            @block.sync
            def _(sync, tj=tj, t0=t0):
                if tj > 0:
                    p.wait(sync, se)
                    p.wait(sync, ve)
                for dt_ in range(DT):
                    p.inc(sync.dma_start(out=xc[:, dt_, :],
                                         in_=xv[:, dt_, t0:t0 + 512]), ld, 16)

            @block.gpsimd
            def _(g, tj=tj, t0=t0):
                if tj > 0:
                    p.wait(g, pe)
                for dt_ in range(DT):
                    p.inc(g.dma_start(out=xbc[:, dt_, :],
                                      in_=xv[:, dt_, t0:t0 + 512]), lg, 16)

            sq_marks = {}
            mm_marks = {}
            for dt_ in range(DT):
                @block.scalar
                def _(scalar, tj=tj, dt_=dt_):
                    if dt_ == 0:
                        p.wait(scalar, ld)
                    if dt_ >= 2:
                        scalar.wait_ge(pe, mm_marks[dt_ - 2])
                    p.inc(scalar.activation(lb.scr[:, dt_ % 2, 0:512],
                                            xc[:, dt_, :], AF.Square), se)
                    sq_marks[dt_] = p.val(se)

                @block.tensor
                def _(tensor, tj=tj, dt_=dt_):
                    if dt_ == 0:
                        p.wait(tensor, se)
                    else:
                        tensor.wait_ge(se, sq_marks[dt_])
                    i = tensor.matmul(ps_all[0:1, (dt_ // 2) * 512:(dt_ // 2) * 512 + 512],
                                      onef[:, 0:1], lb.scr[:, dt_ % 2, 0:512],
                                      start=(dt_ % 2 == 0), stop=(dt_ % 2 == 1))
                    p.inc(i, pe)
                    mm_marks[dt_] = p.val(pe)

            @block.scalar
            def _(scalar, tj=tj):
                p.wait(scalar, pe)
                p.inc(scalar.activation(lb.rows[0:1, 0:2048],
                                        ps_all[0:1, 0:2048], AF.Ln), se)
                p.wait(scalar, se)
                p.inc(scalar.activation(lb.rows[0:1, 0:2048],
                                        lb.rows[0:1, 0:2048], AF.Exp,
                                        scale=cms[0:1, 0:1]), se)

            @block.vector
            def _(vector, tj=tj):
                p.wait(vector, se)
                p.inc(vector.tensor_single_scalar(lb.rows[0:1, 0:2048],
                                                  lb.rows[0:1, 0:2048],
                                                  1e-12, ALU.max), ve)
                p.wait(vector, ve)
                p.inc(vector.reciprocal(lb.rows[0:1, 0:2048],
                                        lb.rows[0:1, 0:2048]), ve)

            @block.tensor
            def _(tensor, tj=tj):
                p.wait(tensor, ve)
                p.wait(tensor, se)
                for h in range(4):
                    i = tensor.matmul(bank(ps_all, 4 + h),
                                      lb.ones_row_f32[0:1, :],
                                      lb.rows[0:1, h * 512:(h + 1) * 512],
                                      start=True, stop=True)
                p.inc(i, pe)

            @block.vector
            def _(vector, tj=tj, t0=t0):
                p.wait(vector, pe)
                for dt_ in range(DT):
                    p.inc(vector.tensor_tensor(
                        rkT[:, dt_, t0 + 1:t0 + 513], xc[:, dt_, :],
                        bank(ps_all, 4 + dt_ // 2), ALU.mult), ve)

            for w in range(2):
                @block.tensor
                def _(tensor, tj=tj, w=w):
                    p.wait(tensor, lg)
                    if w == 0:
                        p.wait(tensor, se)
                    else:
                        p.wait(tensor, se)
                    for jt in range(4 * w, 4 * w + 4):
                        for kt in range(DT):
                            i = tensor.matmul(bank(ps_all, jt % 4),
                                              wwb[:, kt, jt * 128:(jt + 1) * 128],
                                              xbc[:, kt, :],
                                              start=(kt == 0), stop=(kt == 7))
                    p.inc(i, pe)

                @block.scalar
                def _(scalar, tj=tj, w=w):
                    p.wait(scalar, pe)
                    p.inc(scalar.activation(vT_s[:, 4 * w:4 * w + 4, :],
                                            ps_all[:, 0:2048]
                                            .rearrange("p (a b) -> p a b", b=512),
                                            AF.Copy), se)

            for what in range(2):
                @block.tensor
                def _(tensor, tj=tj, t0=t0, what=what):
                    p.wait(tensor, ve)
                    p.wait(tensor, se)
                    for q in range(4):
                        bkb = bank(ps_all, 4 + q).bitcast(BF16)
                        for dt_ in range(DT):
                            if what == 0:
                                src = vT_s[:, dt_, q * 128:(q + 1) * 128]
                            else:
                                src = rkT[:, dt_, t0 + q * 128:t0 + q * 128 + 128]
                            i = tensor.transpose(bkb[:, dt_ * 128:(dt_ + 1) * 128],
                                                 src, eyeb[:, :])
                    p.inc(i, pe)

                @block.scalar
                def _(scalar, tj=tj, what=what):
                    p.wait(scalar, pe)
                    dst = vtm if what == 0 else wtm
                    for q in range(4):
                        bkb = bank(ps_all, 4 + q).bitcast(BF16)
                        i = scalar.activation(dst[:, 4 * tj + q, :],
                                              bkb[:, 0:1024], AF.Copy)
                    p.inc(i, se)

        Pd = PW[0:16, 0:4096].bitcast(F32)[:, 0:1024]
        Qd = PW[0:16, 0:4096].bitcast(F32)[:, 1024:2048]
        Zd = ZW[0:16, 0:4096].bitcast(F32)[:, 0:1024]
        ZTd = ZW[0:16, 0:4096].bitcast(F32)[:, 1024:2048]
        Dzp = PW[0:16, 0:4096].rearrange("p (s j c) -> p s j c", j=4, c=64)
        Bca = ZW[0:16, 0:1024].rearrange("p (k s c) -> p k s c", s=16, c=16)
        DTzb = ZW[0:16, 1024:2048]
        Bfull = ZW[0:16, 2048:3584].rearrange("p (f s c) -> p f s c", s=16, c=16)
        NB = vT_s[0:16, 0:3, :].rearrange("p a b -> p (a b)") \
            .rearrange("p (f s c) -> p f s c", s=16, c=16)
        NB4 = vT_s[0:16, 0:3, :].rearrange("p a b -> p (a b)") \
            .rearrange("p (f cl h c) -> p f cl h c", f=6, h=4, c=16)
        Ssb = vT_s[0:16, 3, 0:256]
        P6 = [(1, 0), (2, 0), (3, 0), (2, 1), (3, 1), (3, 2)]
        gview = ps_all[0:16, 0:4096] \
            .rearrange("p (cl h i c) -> p cl h i c", h=4, i=4, c=64)

        def sl4(ap, i):
            return ap.rearrange("p (cl h i c) -> p cl h i c", h=4, i=4, c=16)[:, :, :, i, :]

        for g in range(8):
            @block.tensor
            def _(tensor, g=g):
                p.wait(tensor, ve)
                p.wait(tensor, se)
                for sl in range(16):
                    c, h = g * 4 + sl // 4, sl % 4
                    for i in range(4):
                        for u in range(2):
                            i_ = tensor.matmul(
                                ps_all[0:16, (sl * 4 + i) * 64:(sl * 4 + i + 1) * 64],
                                rkT[:, 2 * h + u, c * 64 + 16 * i:c * 64 + 16 * i + 16],
                                rkT[:, 2 * h + u, c * 64:c * 64 + 64],
                                start=(u == 0), stop=(u == 1))
                p.inc(i_, pe)

            @block.vector
            def _(vector, g=g):
                p.wait(vector, pe)
                for i in range(4):
                    vector.tensor_tensor(
                        sl4(Pd, i), gview[:, :, :, i, 16 * i:16 * i + 16],
                        CUd[:, :, :].unsqueeze(1).broadcast_to((16, 4, 4, 16)),
                        ALU.mult)
                    vector.tensor_tensor(
                        sl4(Qd, i), gview[:, :, :, i, 16 * i:16 * i + 16],
                        CLd[:, :, :].unsqueeze(1).broadcast_to((16, 4, 4, 16)),
                        ALU.mult)
                for f, (k, j) in enumerate(P6):
                    i_ = vector.tensor_tensor(
                        NB4[:, f, :, :, :], gview[:, :, :, k, 16 * j:16 * j + 16],
                        OD[:, k - j - 1, :, :].unsqueeze(1)
                        .broadcast_to((16, 4, 4, 16)), ALU.mult)
                p.inc(i_, ve)
                p.wait(vector, ve)
                eyc = eyf[:, :].unsqueeze(1).broadcast_to((16, 64, 16))
                vector.tensor_tensor(Zd.rearrange("p (t c) -> p t c", c=16),
                                     Qd.rearrange("p (t c) -> p t c", c=16),
                                     eyc, ALU.add)
                p.inc(vector.tensor_tensor(
                    ZTd.rearrange("p (t c) -> p t c", c=16),
                    Pd.rearrange("p (t c) -> p t c", c=16), eyc, ALU.add), ve)

            @block.tensor
            def _(tensor, g=g):
                p.wait(tensor, ve)
                for sl in range(16):
                    c, h = g * 4 + sl // 4, sl % 4
                    for u in range(2):
                        i_ = tensor.matmul(
                            ps_all[0:64, sl * 64:(sl + 1) * 64],
                            rkT[:, 2 * h + u, c * 64:c * 64 + 64],
                            rkT[:, 2 * h + u, c * 64 + 1:c * 64 + 65],
                            start=(u == 0), stop=(u == 1))
                p.inc(i_, pe)

            @block.vector
            def _(vector, g=g):
                p.wait(vector, pe)
                cupb = CUP[0:64, :, :].rearrange("p a b -> p (a b)") \
                    .unsqueeze(1).broadcast_to((64, 4, 256))
                p.inc(vector.tensor_tensor(
                    ILT[0:64, g * 4:(g + 1) * 4, :],
                    ps_all[0:64, 0:1024].rearrange("p (a b) -> p a b", b=256),
                    cupb, ALU.mult), ve)

            ps16P = ps_all[0:16, 2048:3072]
            ps16Q = ps_all[0:16, 3072:4096]
            for k in range(1, 4):
                @block.tensor
                def _(tensor, g=g, k=k):
                    p.wait(tensor, ve)
                    p.wait(tensor, se)
                    for t in range(64):
                        i_ = tensor.matmul(ps16Q[:, t * 16:(t + 1) * 16],
                                           Pd[0:16, t * 16:(t + 1) * 16],
                                           Qd[0:16, t * 16:(t + 1) * 16],
                                           start=True, stop=True)
                    if k <= 2:
                        for t in range(64):
                            i_ = tensor.matmul(ps16P[:, t * 16:(t + 1) * 16],
                                               Qd[0:16, t * 16:(t + 1) * 16],
                                               Pd[0:16, t * 16:(t + 1) * 16],
                                               start=True, stop=True)
                    p.inc(i_, pe)

                @block.scalar
                def _(scalar, g=g, k=k):
                    p.wait(scalar, pe)
                    i_ = scalar.activation(Qd, ps16Q, AF.Copy)
                    if k <= 2:
                        i_ = scalar.activation(Pd, ps16P, AF.Copy)
                    p.inc(i_, se)

                @block.tensor
                def _(tensor, g=g, k=k):
                    p.wait(tensor, se)
                    for t in range(64):
                        tensor.matmul(ps16P[:, t * 16:(t + 1) * 16],
                                      ZTd[0:16, t * 16:(t + 1) * 16],
                                      Qd[0:16, t * 16:(t + 1) * 16],
                                      start=True, stop=True)
                        i_ = tensor.matmul(ps16Q[:, t * 16:(t + 1) * 16],
                                           Qd[0:16, t * 16:(t + 1) * 16],
                                           ZTd[0:16, t * 16:(t + 1) * 16],
                                           start=True, stop=True)
                    p.inc(i_, pe)

                @block.vector
                def _(vector, g=g, k=k):
                    p.wait(vector, pe)
                    vector.tensor_tensor(Zd, Zd, ps16P, ALU.add)
                    p.inc(vector.tensor_tensor(ZTd, ZTd, ps16Q, ALU.add), ve)

            @block.scalar
            def _(scalar, g=g):
                p.wait(scalar, ve)
                zv = Zd.rearrange("p (s j c) -> p s j c", j=4, c=16)
                for j in range(4):
                    scalar.activation(Dzp[:, :, j, 16 * j:16 * j + 16],
                                      zv[:, :, j, :], AF.Copy)
                p.inc(scalar.activation(DTzb, ZTd, AF.Copy), se)

            for i in range(4):
                @block.scalar
                def _(scalar, g=g, i=i):
                    p.wait(scalar, se)
                    p.wait(scalar, pe)
                    dA = DTzb.rearrange("p (s i c) -> p s i c", i=4, c=16)
                    p.inc(scalar.activation(Bca[:, i, :, :], dA[:, :, i, :],
                                            AF.Copy), se)

                for j in range(i - 1, -1, -1):
                    @block.tensor
                    def _(tensor, g=g, i=i, j=j):
                        p.wait(tensor, se)
                        for sl in range(16):
                            for k in range(j + 1, i + 1):
                                f = P6.index((k, j))
                                i_ = tensor.matmul(
                                    ps_all[0:16, sl * 16:(sl + 1) * 16],
                                    NB[:, f, sl, :], Bca[:, k, sl, :],
                                    start=(k == j + 1), stop=(k == i))
                        p.inc(i_, pe)

                    @block.scalar
                    def _(scalar, g=g, i=i, j=j):
                        p.wait(scalar, pe)
                        p.inc(scalar.activation(Ssb, ps_all[0:16, 0:256],
                                                AF.Copy), se)

                    @block.tensor
                    def _(tensor, g=g, i=i, j=j):
                        p.wait(tensor, se)
                        for sl in range(16):
                            i_ = tensor.matmul(
                                ps_all[0:16, 256 + sl * 16:256 + (sl + 1) * 16],
                                PW[0:16, (sl * 4 + j) * 64 + 16 * j:
                                   (sl * 4 + j) * 64 + 16 * j + 16],
                                Ssb[0:16, sl * 16:(sl + 1) * 16],
                                start=True, stop=True)
                        p.inc(i_, pe)

                    @block.scalar
                    def _(scalar, g=g, i=i, j=j):
                        p.wait(scalar, pe)
                        bv = ps_all[0:16, 256:512] \
                            .rearrange("p (s c) -> p s c", c=16)
                        scalar.activation(Bca[:, j, :, :], bv, AF.Copy)
                        p.inc(scalar.activation(Bfull[:, P6.index((i, j)), :, :],
                                                bv, AF.Copy), se)

            @block.tensor
            def _(tensor, g=g):
                p.wait(tensor, se)
                p.wait(tensor, ve)
                dv = DTzb.rearrange("p (s i c) -> p s i c", i=4, c=16)
                for sl in range(16):
                    o0 = 1024 + sl * 64
                    for i in range(4):
                        i_ = tensor.matmul(
                            ps_all[0:64, o0 + 16 * i:o0 + 16 * i + 16],
                            SELb[0:16, i * 64:(i + 1) * 64], dv[:, sl, i, :],
                            start=True, stop=(i == 0))
                        for j in range(i):
                            i_ = tensor.matmul(
                                ps_all[0:64, o0 + 16 * i:o0 + 16 * i + 16],
                                SELb[0:16, j * 64:(j + 1) * 64],
                                Bfull[:, P6.index((i, j)), sl, :],
                                start=False, stop=(j == i - 1))
                p.inc(i_, pe)

            @block.scalar
            def _(scalar, g=g):
                p.wait(scalar, pe)
                p.inc(scalar.activation(
                    ATv[0:64, g * 4:(g + 1) * 4, :],
                    ps_all[0:64, 1024:2048].rearrange("p (a b) -> p a b", b=256),
                    AF.Copy), se)

        @block.gpsimd
        def _(g):
            p.wait(g, pe)
            for dt_ in range(DT):
                p.inc(g.dma_start(out=wwb[:, dt_, :],
                                  in_=wrt_h[dt_ * 128:(dt_ + 1) * 128, :]),
                      lg, 16)

        @block.vector
        def _(vector):
            p.wait(vector, lg)
            p.wait(vector, se)
            wv = wwb[:, :, :].rearrange("p a b -> p (a b)") \
                .rearrange("p (a b) -> p a b", b=2048)
            p.inc(vector.tensor_tensor(
                wv, wv, gbcb[:, :].unsqueeze(2).broadcast_to((128, 4, 2048)),
                ALU.mult), ve)

        if debug == 1:
            @block.sync
            def _(sync):
                p.wait(sync, se)
                p.wait(sync, ve)
                for dt_ in range(DT):
                    p.inc(sync.dma_start(out=xo[:, dt_, 0:512],
                                         in_=xc[:, dt_, :]), so, 16)

            @block.vector
            def _(vector):
                p.wait(vector, so)
            return p

        @block.tensor
        def _(tensor):
            p.wait(tensor, ve)
            p.wait(tensor, se)
            for h in range(4):
                i = tensor.matmul(bank(ps_all, 4 + h), ones_row_bf[0:1, :],
                                  zrow_bf[0:1, :], start=True, stop=False,
                                  skip_group_check=True)
            p.inc(i, pe)

        @block.scalar
        def _(scalar):
            p.wait(scalar, pe)
            p.inc(scalar.activation(Sb[:, :, :],
                                    ps_all[:, 2048:4096]
                                    .rearrange("p (a b) -> p a b", b=256),
                                    AF.Copy), se)

        for c in range(NCH):
            b = (c % 2) * 64
            pair = c // 2

            @block.vector
            def _(vector, c=c):
                p.wait(vector, pe)
                for h in range(4):
                    i = vector.tensor_tensor(
                        rks[:, h, :, :],
                        rkT[:, 2 * h:2 * h + 2, c * 64 + 1:c * 64 + 65],
                        gpb[:, h * 64:(h + 1) * 64].unsqueeze(1)
                        .broadcast_to((128, 2, 64)), ALU.mult)
                p.inc(i, ve)

            @block.scalar
            def _(scalar, c=c, b=b, pair=pair):
                p.wait(scalar, pe)
                for h in range(4):
                    i = scalar.activation(wkg[0:64, h, :],
                                          wtm[b:b + 64, pair, h * 256:(h + 1) * 256],
                                          AF.Identity,
                                          scale=gcol[b:b + 64, 4 + h:5 + h])
                p.inc(i, se)

            @block.tensor
            def _(tensor, c=c):
                p.wait(tensor, se)
                for h in range(4):
                    for u in range(2):
                        i = tensor.matmul(ps_all[0:64, h * 256:(h + 1) * 256],
                                          rkT[:, 2 * h + u, c * 64:c * 64 + 64],
                                          Sb[:, 2 * h + u, :],
                                          start=(u == 0), stop=(u == 1))
                p.inc(i, pe)

            @block.scalar
            def _(scalar, c=c, b=b):
                p.wait(scalar, pe)
                p.inc(scalar.activation(ub[b:b + 64, :], ps_all[0:64, 0:1024],
                                        AF.Identity, scale=cm1[0:64, 0:1]), se)

            @block.vector
            def _(vector, c=c, b=b, pair=pair):
                p.wait(vector, se)
                p.inc(vector.tensor_tensor(ub[0:64, :], vtm[b:b + 64, pair, :],
                                           ub[b:b + 64, :], ALU.add), ve)

            @block.tensor
            def _(tensor, c=c):
                p.wait(tensor, ve)
                for h in range(4):
                    i = tensor.matmul(ps_all[0:64, h * 256:(h + 1) * 256],
                                      ATv[0:64, c, h * 64:(h + 1) * 64],
                                      ub[0:64, h * 256:(h + 1) * 256],
                                      start=True, stop=True)
                p.inc(i, pe)

            @block.scalar
            def _(scalar, c=c):
                p.wait(scalar, pe)
                p.inc(scalar.activation(vnewb[0:64, :, :],
                                        ps_all[0:64, 0:1024]
                                        .rearrange("p (a b) -> p a b", b=256),
                                        AF.Copy), se)

            @block.tensor
            def _(tensor, c=c):
                p.wait(tensor, se)
                for h in range(4):
                    o0 = 1024 + h * 256
                    tensor.matmul(ps_all[0:64, o0:o0 + 256], rks[:, h, 0, :],
                                  Sb[:, 2 * h, :], start=True, stop=False)
                    tensor.matmul(ps_all[0:64, o0:o0 + 256], rks[:, h, 1, :],
                                  Sb[:, 2 * h + 1, :], start=False, stop=False)
                    i = tensor.matmul(ps_all[0:64, o0:o0 + 256],
                                      ILT[0:64, c, h * 64:(h + 1) * 64],
                                      vnewb[0:64, h, :], start=False, stop=True)
                p.inc(i, pe)

            @block.scalar
            def _(scalar, c=c, b=b, pair=pair):
                p.wait(scalar, pe)
                p.wait(scalar, ve)
                p.inc(scalar.activation(vtm[b:b + 64, pair, :],
                                        ps_all[0:64, 1024:2048], AF.Copy), se)

            @block.vector
            def _(vector, c=c):
                p.wait(vector, pe)
                p.wait(vector, se)
                sv = ps_all[:, 2048:4096].rearrange("p (a b) -> p a b", b=512)
                p.inc(vector.tensor_tensor(
                    sv, sv, gbc[:, 8:12].unsqueeze(2).broadcast_to((128, 4, 512)),
                    ALU.mult), ve)

            @block.tensor
            def _(tensor, c=c):
                p.wait(tensor, ve)
                p.wait(tensor, se)
                for h in range(4):
                    for u in range(2):
                        i = tensor.matmul(
                            ps_all[0:128, (4 + h) * 512 + u * 256:(4 + h) * 512 + u * 256 + 256],
                            wkg[0:64, h, u * 128:(u + 1) * 128],
                            vnewb[0:64, h, :], start=False, stop=False,
                            skip_group_check=True)
                p.inc(i, pe)

            @block.scalar
            def _(scalar, c=c):
                p.wait(scalar, pe)
                p.inc(scalar.activation(Sb[:, :, :],
                                        ps_all[:, 2048:4096]
                                        .rearrange("p (a b) -> p a b", b=256),
                                        AF.Copy), se)

        for tj in range(NTB):
            t0 = tj * 512

            @block.sync
            def _(sync, tj=tj, t0=t0):
                p.wait(sync, pe)
                p.wait(sync, so)
                p.wait(sync, ve)
                for dt_ in range(DT):
                    p.inc(sync.dma_start(out=xc[:, dt_, :],
                                         in_=xv[:, dt_, t0:t0 + 512]), ld, 16)

            for r in range(2):
                @block.tensor
                def _(tensor, tj=tj, r=r):
                    p.wait(tensor, se)
                    p.wait(tensor, ve)
                    for dt_ in range(4 * r, 4 * r + 4):
                        bkb = bank(ps_all, dt_ % 4).bitcast(BF16)
                        for q in range(4):
                            i = tensor.transpose(
                                bkb[:, q * 128:(q + 1) * 128],
                                vtm[:, 4 * tj + q, dt_ * 128:(dt_ + 1) * 128],
                                eyeb[:, :])
                    p.inc(i, pe)

                @block.scalar
                def _(scalar, tj=tj, r=r):
                    p.wait(scalar, pe)
                    for dt_ in range(4 * r, 4 * r + 4):
                        bkb = bank(ps_all, dt_ % 4).bitcast(BF16)
                        i = scalar.activation(vT_s[:, dt_, :], bkb[:, 0:512],
                                              AF.Copy)
                    p.inc(i, se)

            for w in range(2):
                @block.tensor
                def _(tensor, tj=tj, w=w):
                    p.wait(tensor, se)
                    p.wait(tensor, ve)
                    p.wait(tensor, lg)
                    for mt in range(4 * w, 4 * w + 4):
                        for kt in range(DT):
                            i = tensor.matmul(bank(ps_all, 4 + mt % 4),
                                              wwb[:, kt, mt * 128:(mt + 1) * 128],
                                              vT_s[:, kt, :],
                                              start=(kt == 0), stop=(kt == 7))
                    p.inc(i, pe)

                @block.vector
                def _(vector, tj=tj, w=w):
                    p.wait(vector, pe)
                    p.wait(vector, ld)
                    for mt in range(4 * w, 4 * w + 4):
                        i = vector.scalar_tensor_tensor(
                            xc[:, mt, :], xc[:, mt, :], 2.0,
                            bank(ps_all, 4 + mt % 4), ALU.mult, ALU.add)
                    p.inc(i, ve)

            emit_ln_chunk(p, block, lb, xc[:, :, :], xc[:, :, :], sems=sems)

            @block.sync
            def _(sync, tj=tj, t0=t0):
                p.wait(sync, ve)
                for dt_ in range(DT):
                    p.inc(sync.dma_start(out=xo[:, dt_, t0:t0 + 512],
                                         in_=xc[:, dt_, :]), so, 16)

        @block.sync
        def _(sync):
            p.wait(sync, so)
    return p


def l1_attn_in_maps(inputs, xs):
    ww = np.asarray(inputs["L1_attn_Ww"], np.float32)
    wr = np.asarray(inputs["L1_attn_Wr"], np.float32)
    wwt = _f32c(ww.T)
    wrt = _f32c(wr.T)
    gin = np.concatenate([np.asarray(inputs["L1_attn_decay"], np.float32),
                          np.asarray(inputs["L1_attn_log_alpha"], np.float32)]
                         ).reshape(1, 8)
    pidx = np.arange(64)
    difU = np.ascontiguousarray((pidx[None, :] - pidx[:, None]).astype(np.float32))
    mskU = np.triu(np.ones((64, 64), np.float32), 1)
    p16 = np.arange(16)
    d16 = (p16[:, None] - p16[None, :]).astype(np.float32)
    dift = np.ascontiguousarray(np.concatenate(
        [d16, -d16, d16 + 16, d16 + 32, d16 + 48], 1))
    m16 = np.ascontiguousarray(np.concatenate(
        [-np.tril(np.ones((16, 16), np.float32), -1),
         -np.triu(np.ones((16, 16), np.float32), 1)], 1))
    selb = np.zeros((16, 256), BF)
    for j in range(4):
        for r in range(16):
            selb[r, j * 64 + 16 * j + r] = 1
    ioc = (63.0 - (np.arange(128) % 64)).astype(np.float32).reshape(128, 1)
    ior = np.arange(64, dtype=np.float32).reshape(1, 64)
    eyeb = np.eye(128, dtype=BF)
    return [{"x_in": xs[c], "wwt": wwt, "wrt": wrt, "gin": gin, "difU": difU,
             "mskU": mskU, "dift": dift, "m16": m16, "selb": selb,
             "ioc": ioc, "ior": ior, "eyeb": eyeb,
             "eyf": np.eye(16, dtype=np.float32)}
            for c in range(8)]


def _f32c(a):
    return np.ascontiguousarray(a, dtype=np.float32)


def _col(v, n128=None):
    v = np.asarray(v, np.float32)
    n = v.shape[0]
    return np.ascontiguousarray(v.reshape(n // 128, 128).T)


def l1_ffn_in_maps(inputs, xs):
    w1 = _f32c(inputs["L1_ffn_w1"][0])
    w2 = _f32c(inputs["L1_ffn_w2"][0])
    b1c = _col(inputs["L1_ffn_b1"][0])
    b2c = _col(inputs["L1_ffn_b2"][0])
    gam = _f32c(inputs["L1_ffn_gamma"]).reshape(1, D)
    betc = _col(inputs["L1_ffn_beta"])
    return [{"x_in": xs[c], "w1": w1, "b1c": b1c, "w2": w2, "b2c": b2c,
             "gam": gam, "betc": betc} for c in range(8)]


BLOCKS = [
    ("l0_attn", build_l0_attn, l0_attn_in_maps),
    ("l0_ffn", build_l0_ffn, l0_ffn_in_maps),
    ("l1_attn", build_l1_attn, l1_attn_in_maps),
    ("l1_ffn", build_l1_ffn, l1_ffn_in_maps),
    ("l2_attn", build_l2_attn, l2_attn_in_maps),
    ("l2_ffn", build_l2_ffn, l2_ffn_in_maps),
    ("l3_attn", build_l3_attn, l3_attn_in_maps),
    ("l3_ffn", build_l3_ffn, l3_ffn_in_maps),
]


def kernel(**inputs):
    x = np.asarray(inputs["L0_attn_hidden_states"], np.float32)
    xs = [np.ascontiguousarray(x[c].T) for c in range(8)]
    for name, builder, mapper in BLOCKS:
        p = builder()
        in_maps = mapper(inputs, xs)
        res = run_bass_kernel_spmd(p.nc, in_maps, core_ids=list(range(8)))
        xs = [np.asarray(r["x_out"]) for r in res.results]
    out = np.stack([xs[c].T for c in range(8)], 0)
    return np.ascontiguousarray(out, dtype=np.float32)
